# version 11
# baseline (speedup 1.0000x reference)
import math
import contextlib
import numpy as np
import ml_dtypes
import concourse.bass as bass
import concourse.mybir as mybir
from concourse.bass_utils import run_bass_kernel_spmd

F32 = mybir.dt.float32
BF16 = mybir.dt.bfloat16
F16 = mybir.dt.float16
I32 = mybir.dt.int32
AF = mybir.ActivationFunctionType
ALU = mybir.AluOpType
AX = mybir.AxisListType
BF = ml_dtypes.bfloat16

D = 4096
KT = 32
TQ = 1024
TKV = 2048
H = 32
KVH = 8
DFF = 11008
FH_SPLITS = [22, 22, 21, 21]
SCALE = 0.25 / math.sqrt(128.0)
EPS = 1e-5
ENGS = ("sync", "gpsimd", "vector", "scalar", "tensor")
J_FULL = {0: 8, 1: 12}
J_ALL = {0: 12, 1: 16}


class Prog:

    def __init__(self, nc, ctx):
        self.nc = nc
        self._ctx = ctx
        self.q = {e: [] for e in ENGS}
        self.sems = {}
        self.cnt = {}
        for name in ("pe", "dv", "ac"):
            self.sems[name] = ctx.enter_context(nc.semaphore(name))
            self.cnt[name] = 0

    def dsem(self, name):
        if name not in self.sems:
            self.sems[name] = self._ctx.enter_context(self.nc.semaphore(name))
            self.cnt[name] = 0
        return name

    def on(self, eng, fn):
        self.q[eng].append(fn)

    def wait(self, eng, sem, val):
        if val is None or val <= 0:
            return
        h = self.sems[sem]
        self.q[eng].append(lambda e, h=h, v=val: e.wait_ge(h, v))

    def inc(self, sem, k=1):
        self.cnt[sem] += k
        return self.cnt[sem]

    def h(self, name):
        return self.sems[name]

    def barrier(self):
        snap = dict(self.cnt)
        for eng in ENGS:
            for s, v in snap.items():
                self.wait(eng, s, v)


def _finish(nc, p, ctx):
    with nc.Block() as block:
        @block.sync
        def _(e):
            for fn in p.q["sync"]:
                fn(e)

        @block.gpsimd
        def _(e):
            for fn in p.q["gpsimd"]:
                fn(e)

        @block.vector
        def _(e):
            for fn in p.q["vector"]:
                fn(e)

        @block.scalar
        def _(e):
            for fn in p.q["scalar"]:
                fn(e)

        @block.tensor
        def _(e):
            for fn in p.q["tensor"]:
                fn(e)

    ctx.close()
    return nc


def build_program(dbg=None, max_phase=7):
    dbg = set(dbg or [])
    nc = bass.Bass("TRN2", target_bir_lowering=False)
    ctx = contextlib.ExitStack()

    x_kv = nc.dram_tensor("x_kv", [TKV, D], F32, kind="ExternalInput").ap()
    q_codes = nc.dram_tensor("q_codes", [D, D], I32, kind="ExternalInput").ap()
    k_codes = nc.dram_tensor("k_codes", [D, 1024], I32, kind="ExternalInput").ap()
    v_codes = nc.dram_tensor("v_codes", [D, 1024], I32, kind="ExternalInput").ap()
    o_codes = nc.dram_tensor("o_codes", [D, D], I32, kind="ExternalInput").ap()
    q_scales = nc.dram_tensor("q_scales", [KT, D], F32, kind="ExternalInput").ap()
    k_scales = nc.dram_tensor("k_scales", [KT, 1024], F32, kind="ExternalInput").ap()
    v_scales = nc.dram_tensor("v_scales", [KT, 1024], F32, kind="ExternalInput").ap()
    o_scales = nc.dram_tensor("o_scales", [KT, D], F32, kind="ExternalInput").ap()
    w_gate = nc.dram_tensor("w_gate", [DFF, D], F32, kind="ExternalInput").ap()
    w_up = nc.dram_tensor("w_up", [DFF, D], F32, kind="ExternalInput").ap()
    w_down = nc.dram_tensor("w_down", [D, DFF], F32, kind="ExternalInput").ap()
    rope = nc.dram_tensor("rope", [128, 2 * TKV], BF16, kind="ExternalInput").ap()
    sel_in = nc.dram_tensor("sel", [128, TKV], BF16, kind="ExternalInput").ap()
    out = nc.dram_tensor("out", [TQ, D], F32, kind="ExternalOutput").ap()

    ident_c = nc.inline_tensor(np.eye(128, dtype=BF), name="ident_c").ap()
    mm_np = np.where(np.arange(896)[None, :] >= (np.arange(128)[:, None] + 384),
                     np.float16(0.0), np.float16(-30000.0))
    mask_c = nc.inline_tensor(mm_np, name="mask_c").ap()
    ones_c = nc.inline_tensor(np.ones((1, 128), dtype=BF), name="ones_c").ap()
    rot_np = np.zeros((128, 128), dtype=np.float32)
    for _m in range(64):
        rot_np[64 + _m, _m] = -1.0
        rot_np[_m, 64 + _m] = 1.0
    rot_c = nc.inline_tensor(rot_np.astype(BF), name="rot_c").ap()
    eps_c = nc.inline_tensor(np.full((128, 1), EPS, dtype=np.float32), name="eps_c").ap()

    qT_kind = "ExternalOutput" if "qT" in dbg else "Internal"
    yT_kind = "ExternalOutput" if "yT" in dbg else "Internal"
    qT_dram = nc.dram_tensor("qT_scr", [128, H * TQ], BF16, kind=qT_kind).ap()
    yT_dram = nc.dram_tensor("yT_scr", [D, TQ], F32, kind=yT_kind).ap()

    p = Prog(nc, ctx)

    ident = ctx.enter_context(nc.sbuf_tensor("ident", [128, 128], BF16, side="left"))
    onesb = ctx.enter_context(nc.sbuf_tensor("onesb", [1, 128], BF16, side="left"))
    epsb = ctx.enter_context(nc.sbuf_tensor("epsb", [128, 1], F32, side="left"))
    ropeb = ctx.enter_context(nc.sbuf_tensor("ropeb", [128, 2 * TKV], BF16, side="left"))
    rotb = ctx.enter_context(nc.sbuf_tensor("rotb", [128, 128], BF16, side="left"))

    def dbg_dump_sbuf(name, sb, shape, dtype):
        if name not in dbg:
            return
        t = nc.dram_tensor("dbg_" + name, list(shape), dtype, kind="ExternalOutput").ap()
        p.dsem("dbg")
        snap = dict(p.cnt)
        for s, vv in snap.items():
            if s != "dbg":
                p.wait("sync", s, vv)
        p.inc("dbg", 16)
        p.on("sync", lambda e, sb=sb, t=t: e.dma_start(out=t[:, :], in_=sb[:])
             .then_inc(p.h("dbg"), 16))

    p.dsem("cst")
    for dst, src in ((ident, ident_c), (onesb, ones_c), (epsb, eps_c), (ropeb, rope),
                     (rotb, rot_c)):
        p.inc("cst", 16)
        p.on("sync", lambda e, d=dst, s=src: e.dma_start(out=d[:], in_=s[:, :])
             .then_inc(p.h("cst"), 16))
    p.barrier()

    def emit_ln_tile(bufs, row0, dst_view3, trow, ps_T, prev_xn, prev_pe):
        xb, stats, mv, srt, rstd, negmr, xnb = bufs
        p.wait("sync", "dv", prev_xn)
        p.dsem("lnx")
        v_sx = p.inc("lnx", 16)
        p.on("sync", lambda e: e.dma_start(out=xb[:], in_=x_kv[row0:row0 + 128, :])
             .then_inc(p.h("lnx"), 16))
        p.wait("vector", "lnx", v_sx)
        for c in range(8):
            if c < 7:
                p.on("vector", lambda e, c=c: e.bn_stats(out=stats[:, c * 6:(c + 1) * 6],
                                                         in_=xb[:, c * 512:(c + 1) * 512]))
            else:
                v_bs = p.inc("dv")
                p.on("vector", lambda e, c=c: e.bn_stats(out=stats[:, c * 6:(c + 1) * 6],
                                                         in_=xb[:, c * 512:(c + 1) * 512])
                     .then_inc(p.h("dv"), 1))
        p.wait("vector", "dv", v_bs)
        v_mv = p.inc("dv")
        p.on("vector", lambda e: e.bn_aggr(out=mv[:, 0:2], in_=stats[:, 0:48])
             .then_inc(p.h("dv"), 1))
        p.wait("scalar", "dv", v_mv)
        a_srt = p.inc("ac")
        p.on("scalar", lambda e: e.activation(out=srt[:, 0:1], in_=mv[:, 1:2],
                                              func=AF.Sqrt, bias=epsb[:, 0:1])
             .then_inc(p.h("ac"), 1))
        p.wait("vector", "ac", a_srt)
        v_r = p.inc("dv")
        p.on("vector", lambda e: e.reciprocal(out=rstd[:, 0:1], in_=srt[:, 0:1])
             .then_inc(p.h("dv"), 1))
        p.wait("vector", "dv", v_r)
        v_n = p.inc("dv")
        p.on("vector", lambda e: e.tensor_scalar(
            out=negmr[:, 0:1], in0=mv[:, 0:1], scalar1=rstd[:, 0:1], scalar2=-1.0,
            op0=ALU.mult, op1=ALU.mult).then_inc(p.h("dv"), 1))
        p.wait("vector", "dv", v_n)
        p.wait("vector", "pe", prev_pe)
        v_xn = p.inc("dv")
        p.on("vector", lambda e: e.tensor_scalar(
            out=xnb[:], in0=xb[:], scalar1=rstd[:, 0:1], scalar2=negmr[:, 0:1],
            op0=ALU.mult, op1=ALU.add).then_inc(p.h("dv"), 1))
        p.wait("tensor", "dv", v_xn)
        last_pe = None
        for b in range(4):
            for i in range(8):
                k = b * 8 + i
                def t_fn(e, k=k, b=b, i=i):
                    return e.transpose(ps_T[b][:, i * 128:(i + 1) * 128],
                                       xnb[:, k * 128:(k + 1) * 128], ident[:])
                if i < 7:
                    p.on("tensor", t_fn)
                else:
                    last_pe = p.inc("pe")
                    p.on("tensor", lambda e, f=t_fn: f(e).then_inc(p.h("pe"), 1))
            p.wait("vector", "pe", last_pe)
            v_ev = p.inc("dv")
            def ev_fn(e, b=b, trow=trow):
                src = ps_T[b][:, 0:1024].rearrange("p (a n) -> p a n", a=8)
                d = dst_view3[:, b * 8:(b + 1) * 8, trow * 128:(trow + 1) * 128]
                return e.tensor_copy(out=d, in_=src).then_inc(p.h("dv"), 1)
            p.on("vector", ev_fn)
        return v_xn, last_pe, p.cnt["dv"]

    def ln_bufs(c, pfx):
        return (
            c.enter_context(nc.sbuf_tensor(pfx + "xb", [128, D], F32, side="right")),
            c.enter_context(nc.sbuf_tensor(pfx + "st", [128, 48], F32, side="right")),
            c.enter_context(nc.sbuf_tensor(pfx + "mv", [128, 2], F32, side="right")),
            c.enter_context(nc.sbuf_tensor(pfx + "sq", [128, 1], F32, side="right")),
            c.enter_context(nc.sbuf_tensor(pfx + "rs", [128, 1], F32, side="right")),
            c.enter_context(nc.sbuf_tensor(pfx + "nm", [128, 1], F32, side="right")),
            c.enter_context(nc.sbuf_tensor(pfx + "xn", [128, D], BF16, side="right")),
        )

    xnT_q_cm = nc.sbuf_tensor("xnT_q", [128, KT * TQ], BF16, side="right")
    xnT_q = xnT_q_cm.__enter__()
    xnT_q_v3 = xnT_q.ap().rearrange("p (a r) -> p a r", a=KT)

    with contextlib.ExitStack() as lctx:
        bufs = ln_bufs(lctx, "l_")
        ps_T = [lctx.enter_context(nc.psum_tensor(f"psT{b}", [128, 1024], BF16))
                for b in range(4)]
        prev_xn, prev_pe = 0, 0
        for tt in range(8):
            prev_xn, prev_pe, _ = emit_ln_tile(bufs, 1024 + tt * 128, xnT_q_v3, tt,
                                               ps_T, prev_xn, prev_pe)
    p.barrier()
    dbg_dump_sbuf("xnT_q", xnT_q, (128, KT * TQ), BF16)
    if max_phase <= 1:
        return _finish(nc, p, ctx)

    with contextlib.ExitStack() as fctx:
        wT_u = [fctx.enter_context(nc.sbuf_tensor(f"wTu{i}", [128, D], BF16, side="right"))
                for i in range(2)]
        wT_g = [fctx.enter_context(nc.sbuf_tensor(f"wTg{i}", [128, D], BF16, side="right"))
                for i in range(2)]
        h_sb = fctx.enter_context(nc.sbuf_tensor("h_sb", [128, 22 * 1024], BF16, side="right"))
        g_sb = fctx.enter_context(nc.sbuf_tensor("g_sb", [128, 2048], BF16, side="right"))
        wdT = [fctx.enter_context(nc.sbuf_tensor(f"wdT{i}", [128, 2816], BF16, side="right"))
               for i in range(2)]
        yprev = [fctx.enter_context(nc.sbuf_tensor(f"ypv{i}", [128, TQ], F32, side="right"))
                 for i in range(2)]
        yrow = [fctx.enter_context(nc.sbuf_tensor(f"yrw{i}", [128, TQ], F32, side="right"))
                for i in range(2)]
        ps_mm = [fctx.enter_context(nc.psum_tensor(f"psmm{b}", [128, 512], F32))
                 for b in range(8)]

        ffn_mm = {}
        ffn_h = {}

        f0 = 0
        for fh, nfi in enumerate(FH_SPLITS):
            for fi_loc in range(nfi):
                fi = f0 + fi_loc
                bi = fi % 2
                p.wait("gpsimd", "pe", ffn_mm.get(fi - 2))
                su, sg = p.dsem(f"wu{bi}"), p.dsem(f"wg{bi}")
                g1 = p.inc(su, 16)
                p.on("gpsimd", lambda e, fi=fi, bi=bi, su=su: e.dma_start(
                    out=wT_u[bi][:], in_=w_up[fi * 128:(fi + 1) * 128, :])
                    .then_inc(p.h(su), 16))
                g2 = p.inc(sg, 16)
                p.on("gpsimd", lambda e, fi=fi, bi=bi, sg=sg: e.dma_start(
                    out=wT_g[bi][:], in_=w_gate[fi * 128:(fi + 1) * 128, :])
                    .then_inc(p.h(sg), 16))
                p.wait("tensor", su, g1)
                p.wait("tensor", sg, g2)
                p.wait("tensor", "dv", ffn_h.get(fi - 2))
                pb = 4 * (fi % 2)
                v_mm = None
                for k in range(KT):
                    for half, wT in ((0, wT_u[bi]), (1, wT_g[bi])):
                        for tc in range(2):
                            bank = ps_mm[pb + half * 2 + tc]
                            def mm_fn(e, wT=wT, k=k, bank=bank, tc=tc):
                                return e.matmul(
                                    bank[:], wT[:, k * 128:(k + 1) * 128],
                                    xnT_q[:, k * TQ + tc * 512:k * TQ + (tc + 1) * 512],
                                    start=(k == 0), stop=(k == KT - 1))
                            if k == KT - 1 and half == 1 and tc == 1:
                                v_mm = p.inc("pe")
                                p.on("tensor", lambda e, f=mm_fn: f(e).then_inc(p.h("pe"), 1))
                            else:
                                p.on("tensor", mm_fn)
                ffn_mm[fi] = v_mm
                p.wait("scalar", "pe", v_mm)
                p.wait("scalar", "dv", ffn_h.get(fi - 2))
                a_s = None
                for tc in range(2):
                    a_s = p.inc("ac")
                    p.on("scalar", lambda e, tc=tc, pb=pb, fi_loc=fi_loc: e.activation(
                        out=g_sb[:, (fi_loc % 2) * 1024 + tc * 512:
                                 (fi_loc % 2) * 1024 + (tc + 1) * 512],
                        in_=ps_mm[pb + 2 + tc][:],
                        func=AF.Silu).then_inc(p.h("ac"), 1))
                p.wait("vector", "ac", a_s)
                v_h = None
                for tc in range(2):
                    v_h = p.inc("dv")
                    p.on("vector", lambda e, tc=tc, fi_loc=fi_loc, pb=pb: e.tensor_mul(
                        out=h_sb[:, fi_loc * 1024 + tc * 512:fi_loc * 1024 + (tc + 1) * 512],
                        in0=ps_mm[pb + tc][:],
                        in1=g_sb[:, (fi_loc % 2) * 1024 + tc * 512:
                                 (fi_loc % 2) * 1024 + (tc + 1) * 512])
                        .then_inc(p.h("dv"), 1))
                ffn_h[fi] = v_h
            p.wait("tensor", "dv", ffn_h[f0 + nfi - 1])

            dn_mm = {}
            dn_y = {}
            dn_st = {}
            for dm in range(KT):
                bi = dm % 2
                p.wait("gpsimd", "pe", dn_mm.get(dm - 2))
                sd = p.dsem(f"wd{bi}")
                gD = p.inc(sd, 16)
                p.on("gpsimd", lambda e, dm=dm, bi=bi, f0=f0, nfi=nfi, sd=sd: e.dma_start(
                    out=wdT[bi][:, 0:nfi * 128],
                    in_=w_down[dm * 128:(dm + 1) * 128, f0 * 128:(f0 + nfi) * 128])
                    .then_inc(p.h(sd), 16))
                sxp = None
                sy = p.dsem(f"yp{bi}")
                if fh > 0:
                    p.wait("sync", "dv", dn_y.get(dm - 2))
                    sxp = p.inc(sy, 16)
                    p.on("sync", lambda e, dm=dm, bi=bi, sy=sy: e.dma_start(
                        out=yprev[bi][:], in_=yT_dram[dm * 128:(dm + 1) * 128, :])
                        .then_inc(p.h(sy), 16))
                p.wait("tensor", sd, gD)
                p.wait("tensor", "dv", dn_y.get(dm - 2))
                v_mm = None
                for tc in range(2):
                    bank = ps_mm[2 * bi + tc]
                    for jf in range(nfi):
                        def mm_fn(e, jf=jf, bank=bank, tc=tc, nfi=nfi, bi=bi):
                            return e.matmul(
                                bank[:], wdT[bi][:, jf * 128:(jf + 1) * 128],
                                h_sb[:, jf * 1024 + tc * 512:jf * 1024 + (tc + 1) * 512],
                                start=(jf == 0), stop=(jf == nfi - 1))
                        if jf == nfi - 1 and tc == 1:
                            v_mm = p.inc("pe")
                            p.on("tensor", lambda e, f=mm_fn: f(e).then_inc(p.h("pe"), 1))
                        else:
                            p.on("tensor", mm_fn)
                dn_mm[dm] = v_mm
                p.wait("vector", "pe", v_mm)
                if sxp:
                    p.wait("vector", sy, sxp)
                syr = p.dsem(f"yr{bi}")
                p.wait("vector", syr, dn_st.get(dm - 2))
                v_y = None
                for tc in range(2):
                    bank = ps_mm[2 * bi + tc]
                    v_y = p.inc("dv")
                    if fh == 0:
                        p.on("vector", lambda e, bank=bank, tc=tc, bi=bi: e.tensor_copy(
                            out=yrow[bi][:, tc * 512:(tc + 1) * 512], in_=bank[:])
                            .then_inc(p.h("dv"), 1))
                    else:
                        p.on("vector", lambda e, bank=bank, tc=tc, bi=bi: e.tensor_add(
                            out=yrow[bi][:, tc * 512:(tc + 1) * 512], in0=bank[:],
                            in1=yprev[bi][:, tc * 512:(tc + 1) * 512])
                            .then_inc(p.h("dv"), 1))
                dn_y[dm] = v_y
                p.wait("sync", "dv", v_y)
                v_st = p.inc(syr, 16)
                p.on("sync", lambda e, dm=dm, bi=bi, syr=syr: e.dma_start(
                    out=yT_dram[dm * 128:(dm + 1) * 128, :], in_=yrow[bi][:])
                    .then_inc(p.h(syr), 16))
                dn_st[dm] = v_st
            p.barrier()
            f0 += nfi
    p.barrier()
    if max_phase <= 2:
        return _finish(nc, p, ctx)

    with contextlib.ExitStack() as qctx:
        cq = [qctx.enter_context(nc.sbuf_tensor(f"cq{i}", [128, 512], BF16, side="right"))
              for i in range(4)]
        bq = [qctx.enter_context(nc.sbuf_tensor(f"bq{i}", [128, 512], BF16, side="right"))
              for i in range(4)]
        wq = [qctx.enter_context(nc.sbuf_tensor(f"wq{i}", [128, 512], BF16, side="right"))
              for i in range(4)]
        qtmp = qctx.enter_context(nc.sbuf_tensor("qtmp", [128, 512], BF16, side="right"))
        m1 = qctx.enter_context(nc.sbuf_tensor("m1", [128, 512], BF16, side="right"))
        m2 = qctx.enter_context(nc.sbuf_tensor("m2", [128, 512], BF16, side="right"))
        qro = [qctx.enter_context(nc.sbuf_tensor(f"qro{i}", [128, 512], BF16, side="right"))
               for i in range(2)]
        ps_q = [qctx.enter_context(nc.psum_tensor(f"psq{b}", [128, 512], F32))
                for b in range(8)]

        q_dq = {}
        q_mm = {}
        q_ro = {}
        q_rodv = {}
        q_bankfree = {}
        for gq in range(8):
            v_mm_last = None
            for k in range(KT):
                bi = k % 4
                p.wait("gpsimd", "dv", q_dq.get((gq * KT + k) - 4))
                sc_, sb_ = p.dsem(f"cq{bi}"), p.dsem(f"bq{bi}")
                gc = p.inc(sc_, 16)
                p.on("gpsimd", lambda e, gq=gq, k=k, bi=bi, sc_=sc_: e.dma_start(
                    out=cq[bi][:], in_=q_codes[k * 128:(k + 1) * 128,
                                              gq * 512:(gq + 1) * 512])
                    .then_inc(p.h(sc_), 16))
                gs = p.inc(sb_, 16)
                p.on("gpsimd", lambda e, gq=gq, k=k, bi=bi, sb_=sb_: e.dma_start(
                    out=bq[bi][:], in_=q_scales[k:k + 1, gq * 512:(gq + 1) * 512]
                    .broadcast_to([128, 512])).then_inc(p.h(sb_), 16))
                p.wait("vector", sc_, gc)
                p.wait("vector", sb_, gs)
                p.wait("vector", "pe", q_mm.get((gq * KT + k) - 4))
                v_dq = p.inc("dv")
                p.on("vector", lambda e, bi=bi: e.scalar_tensor_tensor(
                    out=wq[bi][:], in0=cq[bi][:], scalar=-8.0, in1=bq[bi][:],
                    op0=ALU.add, op1=ALU.mult).then_inc(p.h("dv"), 1))
                q_dq[gq * KT + k] = v_dq
                p.wait("tensor", "dv", v_dq)
                for h4 in range(4):
                    for tc in range(2):
                        bank = ps_q[h4 * 2 + tc]
                        if k == 0:
                            p.wait("tensor", "dv", q_bankfree.get((gq - 1, h4, tc)))
                        def mm_fn(e, bi=bi, h4=h4, tc=tc, k=k, bank=bank):
                            return e.matmul(
                                bank[:], wq[bi][:, h4 * 128:(h4 + 1) * 128],
                                xnT_q[:, k * TQ + tc * 512:k * TQ + (tc + 1) * 512],
                                start=(k == 0), stop=(k == KT - 1))
                        if h4 == 3 and tc == 1:
                            v_mm_last = p.inc("pe")
                            p.on("tensor", lambda e, f=mm_fn: f(e).then_inc(p.h("pe"), 1))
                        else:
                            p.on("tensor", mm_fn)
                q_mm[gq * KT + k] = v_mm_last
            st_hist = []
            last_rot = 0
            for h4 in range(4):
                h = gq * 4 + h4
                for tc in range(2):
                    bank = ps_q[h4 * 2 + tc]
                    p.wait("scalar", "pe", v_mm_last)
                    p.wait("scalar", "pe", last_rot)
                    p.wait("scalar", "dv", p.cnt["dv"])
                    a_c = p.inc("ac")
                    p.on("scalar", lambda e, bank=bank: e.copy(out=qtmp[:], in_=bank[:])
                         .then_inc(p.h("ac"), 1))
                    p.wait("tensor", "ac", a_c)
                    v_rot = p.inc("pe")
                    last_rot = v_rot
                    p.on("tensor", lambda e, bank=bank: e.matmul(
                        bank[:], rotb[:], qtmp[:], start=True, stop=True)
                        .then_inc(p.h("pe"), 1))
                    p.wait("vector", "ac", a_c)
                    p.wait("vector", "pe", v_rot)
                    col = 1024 + tc * 512
                    bi2 = len(st_hist) % 2
                    sq_ = p.dsem(f"qs{bi2}")
                    if len(st_hist) >= 2:
                        p.wait("vector", sq_, st_hist[-2])
                    p.on("vector", lambda e, col=col: e.tensor_mul(
                        out=m1[:], in0=qtmp[:], in1=ropeb[:, col:col + 512]))
                    v_m2 = p.inc("dv")
                    p.on("vector", lambda e, col=col, bank=bank: e.tensor_mul(
                        out=m2[:], in0=bank[:], in1=ropeb[:, 2048 + col:2048 + col + 512])
                        .then_inc(p.h("dv"), 1))
                    p.wait("vector", "dv", v_m2)
                    q_bankfree[(gq, h4, tc)] = v_m2
                    v_ro = p.inc("dv")
                    p.on("vector", lambda e, bi2=bi2: e.tensor_add(
                        out=qro[bi2][:], in0=m1[:], in1=m2[:])
                        .then_inc(p.h("dv"), 1))
                    p.wait("sync", "dv", v_ro)
                    v_st = p.inc(sq_, 16)
                    p.on("sync", lambda e, h=h, tc=tc, bi2=bi2, sq_=sq_: e.dma_start(
                        out=qT_dram[:, h * 1024 + tc * 512:h * 1024 + (tc + 1) * 512],
                        in_=qro[bi2][:]).then_inc(p.h(sq_), 16))
                    st_hist.append(v_st)
            q_ro[gq] = p.cnt["ac"]
            q_rodv[gq] = p.cnt["dv"]
    p.barrier()
    if max_phase <= 3:
        return _finish(nc, p, ctx)
    xnT_q_cm.__exit__(None, None, None)

    kv_stack = contextlib.ExitStack()
    kT = kv_stack.enter_context(nc.sbuf_tensor("kT", [128, KVH * TKV], BF16, side="left"))
    vsb = kv_stack.enter_context(nc.sbuf_tensor("vsb", [128, KVH * 16 * 128], BF16,
                                                side="left"))
    maskb = kv_stack.enter_context(nc.sbuf_tensor("maskb", [128, 896], F16, side="left"))
    selb = kv_stack.enter_context(nc.sbuf_tensor("selb", [128, TKV], BF16, side="left"))
    p.dsem("cst2")
    for dst, src in ((maskb, mask_c), (selb, sel_in)):
        p.inc("cst2", 16)
        p.on("sync", lambda e, d=dst, s=src: e.dma_start(out=d[:], in_=s[:, :])
             .then_inc(p.h("cst2"), 16))
    vsb_v3 = vsb.ap().rearrange("p (a r) -> p a r", a=KVH)

    with contextlib.ExitStack() as kctx:
        xnT_cc = kctx.enter_context(nc.sbuf_tensor("xnT_cc", [128, KT * 512], BF16,
                                                   side="right"))
        xnT_cc_v3 = xnT_cc.ap().rearrange("p (a r) -> p a r", a=KT)
        ck = [kctx.enter_context(nc.sbuf_tensor(f"ck{i}", [128, 1024], BF16, side="right"))
              for i in range(4)]
        bk = [kctx.enter_context(nc.sbuf_tensor(f"bk{i}", [128, 1024], BF16, side="right"))
              for i in range(4)]
        wk = [kctx.enter_context(nc.sbuf_tensor(f"wk{i}", [128, 1024], BF16, side="right"))
              for i in range(4)]
        ktmp = kctx.enter_context(nc.sbuf_tensor("ktmp", [128, 512], BF16, side="right"))
        km1 = kctx.enter_context(nc.sbuf_tensor("km1", [128, 512], BF16, side="right"))
        km2 = kctx.enter_context(nc.sbuf_tensor("km2", [128, 512], BF16, side="right"))
        lnb = ln_bufs(kctx, "k_")

        kv_dq = {}
        kv_mm = {}
        dq_i = 0
        for cc in range(4):
            if cc < 2:
                with contextlib.ExitStack() as tctx:
                    ps_T = [tctx.enter_context(nc.psum_tensor(f"psKT{cc}{b}", [128, 1024],
                                                              BF16)) for b in range(4)]
                    prev_xn, prev_pe = 0, 0
                    for t4 in range(4):
                        prev_xn, prev_pe, _ = emit_ln_tile(
                            lnb, cc * 512 + t4 * 128, xnT_cc_v3, t4, ps_T,
                            prev_xn, prev_pe)
                p.barrier()

                def xn_slice(k, cc=cc):
                    return xnT_cc[:, k * 512:(k + 1) * 512]
                def xn_block(k, tb, cc=cc):
                    return xnT_cc[:, k * 512 + tb * 128:k * 512 + (tb + 1) * 128]
            else:
                def xn_slice(k, cc=cc):
                    return xnT_q[:, k * TQ + (cc - 2) * 512:k * TQ + (cc - 2) * 512 + 512]
                def xn_block(k, tb, cc=cc):
                    b0 = k * TQ + (cc - 2) * 512 + tb * 128
                    return xnT_q[:, b0:b0 + 128]
            if cc >= 2:
                with contextlib.ExitStack() as tctx:
                    ps_T = [tctx.enter_context(nc.psum_tensor(f"psKT{cc}{b}", [128, 1024],
                                                              BF16)) for b in range(4)]
                    prev_xn, prev_pe = 0, 0
                    for t4 in range(4):
                        prev_xn, prev_pe, _ = emit_ln_tile(
                            lnb, cc * 512 + t4 * 128, xnT_cc_v3, t4, ps_T,
                            prev_xn, prev_pe)
                p.barrier()

                def xn_slice(k, cc=cc):
                    return xnT_cc[:, k * 512:(k + 1) * 512]
                def xn_block(k, tb, cc=cc):
                    return xnT_cc[:, k * 512 + tb * 128:k * 512 + (tb + 1) * 128]

            with contextlib.ExitStack() as pctx:
                ps_kv = [pctx.enter_context(nc.psum_tensor(f"pskv{cc}{b}", [128, 512], F32))
                         for b in range(8)]
                v_mm_last = None
                for k in range(KT):
                    bi = k % 4
                    p.wait("gpsimd", "dv", kv_dq.get(dq_i - 4))
                    sc_, sb_ = p.dsem(f"ck{bi}"), p.dsem(f"bk{bi}")
                    gc = p.inc(sc_, 16)
                    p.on("gpsimd", lambda e, k=k, bi=bi, sc_=sc_: e.dma_start(
                        out=ck[bi][:], in_=k_codes[k * 128:(k + 1) * 128, :])
                        .then_inc(p.h(sc_), 16))
                    gs = p.inc(sb_, 16)
                    p.on("gpsimd", lambda e, k=k, bi=bi, sb_=sb_: e.dma_start(
                        out=bk[bi][:], in_=k_scales[k:k + 1, :].broadcast_to([128, 1024]))
                        .then_inc(p.h(sb_), 16))
                    p.wait("vector", sc_, gc)
                    p.wait("vector", sb_, gs)
                    p.wait("vector", "pe", kv_mm.get(dq_i - 4))
                    v_dq = p.inc("dv")
                    p.on("vector", lambda e, bi=bi: e.scalar_tensor_tensor(
                        out=wk[bi][:], in0=ck[bi][:], scalar=-8.0, in1=bk[bi][:],
                        op0=ALU.add, op1=ALU.mult).then_inc(p.h("dv"), 1))
                    kv_dq[dq_i] = v_dq
                    p.wait("tensor", "dv", v_dq)
                    for g in range(8):
                        def mm_fn(e, bi=bi, g=g, k=k):
                            return e.matmul(ps_kv[g][:], wk[bi][:, g * 128:(g + 1) * 128],
                                            xn_slice(k), start=(k == 0), stop=(k == KT - 1))
                        if g == 7:
                            v_mm_last = p.inc("pe")
                            p.on("tensor", lambda e, f=mm_fn: f(e).then_inc(p.h("pe"), 1))
                        else:
                            p.on("tensor", mm_fn)
                    kv_mm[dq_i] = v_mm_last
                    dq_i += 1
                v_ro_last = None
                last_rot = 0
                for g in range(8):
                    p.wait("scalar", "pe", v_mm_last)
                    p.wait("scalar", "pe", last_rot)
                    if v_ro_last:
                        p.wait("scalar", "dv", v_ro_last)
                    a_c = p.inc("ac")
                    p.on("scalar", lambda e, g=g: e.copy(out=ktmp[:], in_=ps_kv[g][:])
                         .then_inc(p.h("ac"), 1))
                    p.wait("tensor", "ac", a_c)
                    v_rot = p.inc("pe")
                    last_rot = v_rot
                    p.on("tensor", lambda e, g=g: e.matmul(
                        ps_kv[g][:], rotb[:], ktmp[:], start=True, stop=True)
                        .then_inc(p.h("pe"), 1))
                    p.wait("vector", "ac", a_c)
                    p.wait("vector", "pe", v_rot)
                    col = cc * 512
                    dst0 = g * TKV + cc * 512
                    p.on("vector", lambda e, col=col: e.tensor_mul(
                        out=km1[:], in0=ktmp[:], in1=ropeb[:, col:col + 512]))
                    v_m2 = p.inc("dv")
                    p.on("vector", lambda e, col=col, g=g: e.tensor_mul(
                        out=km2[:], in0=ps_kv[g][:], in1=ropeb[:, 2048 + col:2048 + col + 512])
                        .then_inc(p.h("dv"), 1))
                    p.wait("vector", "dv", v_m2)
                    v_ro_last = p.inc("dv")
                    p.on("vector", lambda e, dst0=dst0: e.tensor_add(
                        out=kT[:, dst0:dst0 + 512], in0=km1[:], in1=km2[:])
                        .then_inc(p.h("dv"), 1))
                p.wait("tensor", "ac", p.cnt["ac"])
                p.wait("tensor", "dv", p.cnt["dv"])
                for dvs in range(2):
                    v_mm_last = None
                    for k in range(KT):
                        bi = k % 4
                        p.wait("gpsimd", "dv", kv_dq.get(dq_i - 4))
                        sc_, sb_ = p.dsem(f"ck{bi}"), p.dsem(f"bk{bi}")
                        gc = p.inc(sc_, 16)
                        p.on("gpsimd", lambda e, k=k, bi=bi, dvs=dvs, sc_=sc_: e.dma_start(
                            out=ck[bi][:, 0:512],
                            in_=v_codes[k * 128:(k + 1) * 128, dvs * 512:(dvs + 1) * 512])
                            .then_inc(p.h(sc_), 16))
                        gs = p.inc(sb_, 16)
                        p.on("gpsimd", lambda e, k=k, bi=bi, dvs=dvs, sb_=sb_: e.dma_start(
                            out=bk[bi][:, 0:512],
                            in_=v_scales[k:k + 1, dvs * 512:(dvs + 1) * 512]
                            .broadcast_to([128, 512])).then_inc(p.h(sb_), 16))
                        p.wait("vector", sc_, gc)
                        p.wait("vector", sb_, gs)
                        p.wait("vector", "pe", kv_mm.get(dq_i - 4))
                        v_dq = p.inc("dv")
                        p.on("vector", lambda e, bi=bi: e.scalar_tensor_tensor(
                            out=wk[bi][:, 0:512], in0=ck[bi][:, 0:512], scalar=-8.0,
                            in1=bk[bi][:, 0:512], op0=ALU.add, op1=ALU.mult)
                            .then_inc(p.h("dv"), 1))
                        kv_dq[dq_i] = v_dq
                        p.wait("tensor", "dv", v_dq)
                        for tb in range(4):
                            bank = ps_kv[dvs * 4 + tb]
                            def mm_fn(e, bi=bi, tb=tb, k=k, bank=bank):
                                return e.matmul(bank[:], xn_block(k, tb), wk[bi][:, 0:512],
                                                start=(k == 0), stop=(k == KT - 1))
                            if tb == 3:
                                v_mm_last = p.inc("pe")
                                p.on("tensor", lambda e, f=mm_fn: f(e)
                                     .then_inc(p.h("pe"), 1))
                            else:
                                p.on("tensor", mm_fn)
                        kv_mm[dq_i] = v_mm_last
                        dq_i += 1
                    p.wait("vector", "pe", v_mm_last)
                    v_ev = None
                    for tb in range(4):
                        j = cc * 4 + tb
                        bank = ps_kv[dvs * 4 + tb]
                        v_ev = p.inc("dv")
                        def ev_fn(e, bank=bank, j=j, dvs=dvs):
                            src = bank[:].rearrange("p (a n) -> p a n", a=4)
                            d = vsb_v3[:, dvs * 4:(dvs + 1) * 4, j * 128:(j + 1) * 128]
                            return e.tensor_copy(out=d, in_=src).then_inc(p.h("dv"), 1)
                        p.on("vector", ev_fn)
                    p.wait("tensor", "dv", v_ev)
            p.barrier()
    p.barrier()
    dbg_dump_sbuf("kT", kT, (128, KVH * TKV), BF16)
    dbg_dump_sbuf("vsb", vsb, (128, KVH * 16 * 128), BF16)
    if max_phase <= 4:
        kv_stack.close()
        return _finish(nc, p, ctx)

    an_sb = ctx.enter_context(nc.sbuf_tensor("an_sb", [128, H * TQ], BF16, side="right"))

    with contextlib.ExitStack() as actx:
        qh_sb = [actx.enter_context(nc.sbuf_tensor(f"qh{i}", [128, TQ], BF16, side="right"))
                 for i in range(8)]
        s16 = [actx.enter_context(nc.sbuf_tensor(f"s16_{i}", [128, 16 * 512], F16,
                                                 side="right")) for i in range(2)]
        s16m = actx.enter_context(nc.sbuf_tensor("s16m", [128, 512], F16, side="right"))
        rm_all = actx.enter_context(nc.sbuf_tensor("rm_all", [128, 32], BF16, side="right"))
        rm_f = actx.enter_context(nc.sbuf_tensor("rm_f", [128, 1], BF16, side="right"))
        bmax = actx.enter_context(nc.sbuf_tensor("bmax", [1, 1], BF16, side="right"))
        bias_h = [actx.enter_context(nc.sbuf_tensor(f"bh{i}", [128, 1], F32, side="right"))
                  for i in range(8)]
        exp4 = actx.enter_context(nc.sbuf_tensor("exp4", [128, 4 * 512], BF16,
                                                 side="right"))
        recip = actx.enter_context(nc.sbuf_tensor("rcp", [128, 512], F32, side="right"))
        ps_A = [actx.enter_context(nc.psum_tensor(f"psA{i}", [128, 512], F32))
                for i in range(2)]
        ps_sum = [actx.enter_context(nc.psum_tensor(f"psM{i}", [128, 512], F32))
                  for i in range(2)]
        ps_pv = [actx.enter_context(nc.psum_tensor(f"psP{i}", [128, 512], F32))
                 for i in range(2)]

        a_ev = {}
        a_rm = {}
        s16_ev = {}
        s16_rd = {}
        runA = [0]
        b_exp = {}
        b_mm = {}
        runB = [0]
        bias_ac = {}
        qh_done = {}
        qh_lastuse = {}

        def prefetch_qh(g):
            if g >= KVH:
                return
            for h4 in range(4):
                hh = g * 4 + h4
                slot = h4 + 4 * (g % 2)
                sq_ = p.dsem(f"qh{slot}")
                p.wait("sync", "pe", qh_lastuse.get((g - 2, h4)))
                v_q = p.inc(sq_, 16)
                p.on("sync", lambda e, hh=hh, slot=slot, sq_=sq_: e.dma_start(
                    out=qh_sb[slot][:], in_=qT_dram[:, hh * TQ:(hh + 1) * TQ])
                    .then_inc(p.h(sq_), 16))
                qh_done[(g, h4)] = (sq_, v_q)

        def emit_A(g, h4, c):
            slot = h4 + 4 * (g % 2)
            sq_, v_q = qh_done[(g, h4)]
            p.wait("tensor", sq_, v_q)
            for j in range(J_ALL[c]):
                t = runA[0]
                bank = ps_A[t % 2]
                dep = a_ev.get(t - 2)
                if dep:
                    p.wait("tensor", dep[0], dep[1])
                v_pe = p.inc("pe")
                p.on("tensor", lambda e, bank=bank, g=g, j=j, slot=slot, c=c:
                     e.matmul(bank[:],
                              kT[:, g * TKV + j * 128:g * TKV + (j + 1) * 128],
                              qh_sb[slot][:, c * 512:(c + 1) * 512],
                              start=True, stop=True)
                     .then_inc(p.h("pe"), 1))
                qh_lastuse[(g, h4)] = v_pe
                eng = "vector" if (t % 2 == 0) else "scalar"
                p.wait(eng, "pe", v_pe)
                p.wait(eng, "ac", s16_rd.get((c, j)))
                if eng == "vector":
                    v_e = p.inc("dv")
                    p.on("vector", lambda e, bank=bank, c=c, j=j: e.tensor_copy(
                        out=s16[c % 2][:, j * 512:(j + 1) * 512], in_=bank[:])
                        .then_inc(p.h("dv"), 1))
                    a_ev[t] = ("dv", v_e)
                    s16_ev[(c, j)] = ("dv", v_e)
                    v_rm = p.inc("dv")
                    p.on("vector", lambda e, c=c, j=j: e.reduce_max(
                        out=rm_all[:, j:j + 1] if c == 0 else rm_all[:, 12 + j:13 + j],
                        in_=s16[c % 2][:, j * 512:(j + 1) * 512], axis=AX.X)
                        .then_inc(p.h("dv"), 1))
                else:
                    a_e = p.inc("ac")
                    p.on("scalar", lambda e, bank=bank, c=c, j=j: e.copy(
                        out=s16[c % 2][:, j * 512:(j + 1) * 512], in_=bank[:])
                        .then_inc(p.h("ac"), 1))
                    a_ev[t] = ("ac", a_e)
                    s16_ev[(c, j)] = ("ac", a_e)
                    p.wait("vector", "ac", a_e)
                    v_rm = p.inc("dv")
                    p.on("vector", lambda e, c=c, j=j: e.reduce_max(
                        out=rm_all[:, j:j + 1] if c == 0 else rm_all[:, 12 + j:13 + j],
                        in_=s16[c % 2][:, j * 512:(j + 1) * 512], axis=AX.X)
                        .then_inc(p.h("dv"), 1))
                a_rm[t] = v_rm
                runA[0] = t + 1

        def emit_bias(g, h4):
            slot = h4 + 4 * (g % 2)
            p.wait("vector", "dv", a_rm[runA[0] - 1])
            v_rf = p.inc("dv")
            p.on("vector", lambda e: e.reduce_max(
                out=rm_f[:, 0:1], in_=rm_all[:, 0:28], axis=AX.X)
                .then_inc(p.h("dv"), 1))
            p.wait("tensor", "dv", v_rf)
            bankT = ps_A[runA[0] % 2]
            dep = a_ev.get(runA[0] - 2)
            if dep:
                p.wait("tensor", dep[0], dep[1])
            v_pe = p.inc("pe")
            p.on("tensor", lambda e, bankT=bankT: e.transpose(
                bankT.ap().bitcast(BF16)[0:1, 0:128], rm_f[:, 0:1], ident[:])
                .then_inc(p.h("pe"), 1))
            p.wait("vector", "pe", v_pe)
            v_bm = p.inc("dv")
            p.on("vector", lambda e, bankT=bankT: e.reduce_max(
                out=bmax[0:1, 0:1], in_=bankT.ap().bitcast(BF16)[0:1, 0:128],
                axis=AX.X).then_inc(p.h("dv"), 1))
            p.wait("tensor", "dv", v_bm)
            v_pe2 = p.inc("pe")
            p.on("tensor", lambda e, bankT=bankT: e.matmul(
                bankT[:, 0:1], onesb[0:1, :], bmax[0:1, 0:1],
                start=True, stop=True).then_inc(p.h("pe"), 1))
            p.wait("scalar", "pe", v_pe2)
            a_b = p.inc("ac")
            p.on("scalar", lambda e, slot=slot, bankT=bankT: e.activation(
                out=bias_h[slot][:, 0:1], in_=bankT[:, 0:1], func=AF.Identity,
                scale=-SCALE).then_inc(p.h("ac"), 1))
            bias_ac[(g, h4)] = a_b
            a_ev[runA[0] - 2] = ("ac", a_b)

        def emit_B(g, h4, c, ms):
            slot = h4 + 4 * (g % 2)
            n_all = J_ALL[c]
            n_full = J_FULL[c]
            p.wait("scalar", "ac", bias_ac[(g, h4)])
            prev_diag = None
            for j in range(n_all):
                t = runB[0]
                se_slot = t % 4
                diag = j >= n_full
                sev = s16_ev.get((c, j))
                if diag:
                    s_off = 384 - (j - n_full) * 128
                    p.wait("vector", "ac", prev_diag or 0)
                    if sev:
                        p.wait("vector", sev[0], sev[1])
                    v_m = p.inc("dv")
                    p.on("vector", lambda e, c=c, j=j, s_off=s_off: e.tensor_add(
                        out=s16m[:], in0=s16[c % 2][:, j * 512:(j + 1) * 512],
                        in1=maskb[:, s_off:s_off + 512]).then_inc(p.h("dv"), 1))
                    p.wait("scalar", "dv", v_m)
                elif sev:
                    p.wait("scalar", sev[0], sev[1])
                p.wait("scalar", "pe", b_mm.get(t - 4))
                a_e = p.inc("ac")
                if diag:
                    p.on("scalar", lambda e, se_slot=se_slot, slot=slot: e.activation(
                        out=exp4[:, se_slot * 512:(se_slot + 1) * 512], in_=s16m[:],
                        func=AF.Exp, bias=bias_h[slot][:, 0:1], scale=SCALE)
                        .then_inc(p.h("ac"), 1))
                    prev_diag = a_e
                else:
                    p.on("scalar", lambda e, se_slot=se_slot, slot=slot, c=c, j=j:
                         e.activation(
                             out=exp4[:, se_slot * 512:(se_slot + 1) * 512],
                             in_=s16[c % 2][:, j * 512:(j + 1) * 512],
                             func=AF.Exp, bias=bias_h[slot][:, 0:1], scale=SCALE)
                         .then_inc(p.h("ac"), 1))
                b_exp[t] = a_e
                s16_rd[(c, j)] = a_e
                p.wait("tensor", "ac", a_e)
                first = (j == 0)
                lastj = (j == n_all - 1)
                p.on("tensor", lambda e, j=j, se_slot=se_slot, first=first, lastj=lastj,
                     ms=ms: e.matmul(
                         ms[0][:], selb[:, j * 128:(j + 1) * 128],
                         exp4[:, se_slot * 512:(se_slot + 1) * 512],
                         start=first, stop=lastj))
                v_mm = p.inc("pe")
                p.on("tensor", lambda e, j=j, g=g, se_slot=se_slot, first=first,
                     lastj=lastj, ms=ms: e.matmul(
                         ms[1][:],
                         vsb[:, (g * 16 + j) * 128:(g * 16 + j + 1) * 128],
                         exp4[:, se_slot * 512:(se_slot + 1) * 512],
                         start=first, stop=lastj)
                     .then_inc(p.h("pe"), 1))
                b_mm[t] = v_mm
                runB[0] = t + 1
            h = g * 4 + h4
            p.wait("vector", "pe", b_mm[runB[0] - 1])
            v_rc = p.inc("dv")
            p.on("vector", lambda e, ms=ms: e.reciprocal(out=recip[:], in_=ms[0][:])
                 .then_inc(p.h("dv"), 1))
            p.wait("vector", "dv", v_rc)
            v_fin = p.inc("dv")
            p.on("vector", lambda e, h=h, c=c, ms=ms: e.tensor_mul(
                out=an_sb[:, h * TQ + c * 512:h * TQ + (c + 1) * 512],
                in0=ms[1][:], in1=recip[:]).then_inc(p.h("dv"), 1))
            return v_fin

        prefetch_qh(0)
        prefetch_qh(1)
        ms_free = {0: 0, 1: 0}
        mi = 0
        for g in range(KVH):
            if g == 0:
                emit_A(0, 0, 0)
                emit_A(0, 0, 1)
                emit_bias(0, 0)
            for h4 in range(4):
                nh_g, nh_h = (g, h4 + 1) if h4 < 3 else (g + 1, 0)
                for c in range(2):
                    mb = mi % 2
                    p.wait("tensor", "dv", ms_free[mb])
                    ms = (ps_sum[mb], ps_pv[mb])
                    v_fin = emit_B(g, h4, c, ms)
                    ms_free[mb] = v_fin
                    mi += 1
                    if nh_g < KVH:
                        emit_A(nh_g, nh_h, c)
                if nh_g < KVH:
                    emit_bias(nh_g, nh_h)
            prefetch_qh(g + 2)
    p.barrier()
    kv_stack.close()
    dbg_dump_sbuf("an_sb", an_sb, (128, H * TQ), BF16)
    if max_phase <= 5:
        return _finish(nc, p, ctx)

    branch_tok = ctx.enter_context(nc.sbuf_tensor("br_tok", [128, 8 * D], BF16,
                                                  side="left"))
    branch_v3 = branch_tok.ap().rearrange("p (a r) -> p a r", a=8)

    with contextlib.ExitStack() as octx:
        co = [octx.enter_context(nc.sbuf_tensor(f"co{i}", [128, D], BF16, side="right"))
              for i in range(2)]
        bo = [octx.enter_context(nc.sbuf_tensor(f"bo{i}", [128, D], BF16, side="right"))
              for i in range(2)]
        wo = [octx.enter_context(nc.sbuf_tensor(f"wo{i}", [128, D], BF16, side="right"))
              for i in range(2)]
        yt_sb = [octx.enter_context(nc.sbuf_tensor(f"yt{i}", [128, TQ], F32, side="right"))
                 for i in range(2)]
        sfm = octx.enter_context(nc.sbuf_tensor("sfm", [128, TQ], BF16, side="right"))
        ps_o = [octx.enter_context(nc.psum_tensor(f"pso{i}", [128, 512], F32))
                for i in range(4)]
        ps_bt = [octx.enter_context(nc.psum_tensor(f"psbt{i}", [128, 1024], BF16))
                 for i in range(2)]

        o_dq = {}
        o_mm = {}
        o_bt = {}
        o_sfm = {}
        for dm in range(KT):
            bi = dm % 2
            p.wait("gpsimd", "dv", o_dq.get(dm - 2))
            sc_, sb_ = p.dsem(f"co{bi}"), p.dsem(f"bo{bi}")
            gC = p.inc(sc_, 16)
            p.on("gpsimd", lambda e, dm=dm, bi=bi, sc_=sc_: e.dma_start(
                out=co[bi][:], in_=o_codes[dm * 128:(dm + 1) * 128, :])
                .then_inc(p.h(sc_), 16))
            gS = p.inc(sb_, 16)
            p.on("gpsimd", lambda e, dm=dm, bi=bi, sb_=sb_: e.dma_start(
                out=bo[bi][:], in_=o_scales[dm:dm + 1, :].broadcast_to([128, D]))
                .then_inc(p.h(sb_), 16))
            p.wait("vector", sc_, gC)
            p.wait("vector", sb_, gS)
            p.wait("vector", "pe", o_mm.get(dm - 2))
            v_dq = p.inc("dv")
            p.on("vector", lambda e, bi=bi: e.scalar_tensor_tensor(
                out=wo[bi][:], in0=co[bi][:], scalar=-8.0, in1=bo[bi][:],
                op0=ALU.add, op1=ALU.mult).then_inc(p.h("dv"), 1))
            o_dq[dm] = v_dq
            p.wait("sync", "dv", o_sfm.get(dm - 2))
            syt = p.dsem(f"yt{bi}")
            sxp = p.inc(syt, 16)
            p.on("sync", lambda e, dm=dm, bi=bi, syt=syt: e.dma_start(
                out=yt_sb[bi][:], in_=yT_dram[dm * 128:(dm + 1) * 128, :])
                .then_inc(p.h(syt), 16))
            p.wait("tensor", "dv", v_dq)
            p.wait("tensor", "dv", o_sfm.get(dm - 2))
            v_mm = None
            for a in range(KT):
                for tc in range(2):
                    bank = ps_o[(bi % 2) * 2 + tc]
                    def mm_fn(e, bi=bi, a=a, tc=tc, bank=bank):
                        return e.matmul(
                            bank[:], wo[bi][:, a * 128:(a + 1) * 128],
                            an_sb[:, a * TQ + tc * 512:a * TQ + (tc + 1) * 512],
                            start=(a == 0), stop=(a == KT - 1))
                    if a == KT - 1 and tc == 1:
                        v_mm = p.inc("pe")
                        p.on("tensor", lambda e, f=mm_fn: f(e).then_inc(p.h("pe"), 1))
                    else:
                        p.on("tensor", mm_fn)
            o_mm[dm] = v_mm
            p.wait("vector", "pe", v_mm)
            p.wait("vector", syt, sxp)
            p.wait("vector", "pe", o_bt.get(dm - 1, None) and None or None)
            v_s = None
            for tc in range(2):
                v_s = p.inc("dv")
                p.on("vector", lambda e, bi=bi, tc=tc: e.tensor_add(
                    out=sfm[:, tc * 512:(tc + 1) * 512], in0=ps_o[(bi % 2) * 2 + tc][:],
                    in1=yt_sb[bi][:, tc * 512:(tc + 1) * 512])
                    .then_inc(p.h("dv"), 1))
            o_sfm[dm] = v_s
            p.wait("tensor", "dv", v_s)
            p.wait("tensor", "dv", o_bt.get(dm - 2))
            v_pe = None
            for i in range(8):
                def t_fn(e, i=i, bi=bi):
                    return e.transpose(ps_bt[bi][:, i * 128:(i + 1) * 128],
                                       sfm[:, i * 128:(i + 1) * 128], ident[:])
                if i < 7:
                    p.on("tensor", t_fn)
                else:
                    v_pe = p.inc("pe")
                    p.on("tensor", lambda e, f=t_fn: f(e).then_inc(p.h("pe"), 1))
            p.wait("vector", "pe", v_pe)
            v_bt = p.inc("dv")
            def bt_fn(e, dm=dm, bi=bi):
                src = ps_bt[bi][:, 0:1024].rearrange("p (a n) -> p a n", a=8)
                d = branch_v3[:, :, dm * 128:(dm + 1) * 128]
                return e.tensor_copy(out=d, in_=src).then_inc(p.h("dv"), 1)
            p.on("vector", bt_fn)
            o_bt[dm] = v_bt
    p.barrier()
    if max_phase <= 6:
        return _finish(nc, p, ctx)

    with contextlib.ExitStack() as zctx:
        xq = [zctx.enter_context(nc.sbuf_tensor(f"xq{i}", [128, D], F32, side="right"))
              for i in range(2)]
        ob = [zctx.enter_context(nc.sbuf_tensor(f"ob{i}", [128, D], F32, side="right"))
              for i in range(2)]
        prev_add = {}
        prev_st = {}
        for tt in range(8):
            bi = tt % 2
            sxq, sos = p.dsem(f"xq{bi}"), p.dsem(f"os{bi}")
            p.wait("sync", "dv", prev_add.get(tt - 2))
            sxx = p.inc(sxq, 16)
            p.on("sync", lambda e, tt=tt, bi=bi, sxq=sxq: e.dma_start(
                out=xq[bi][:], in_=x_kv[1024 + tt * 128:1024 + (tt + 1) * 128, :])
                .then_inc(p.h(sxq), 16))
            p.wait("vector", sxq, sxx)
            p.wait("vector", sos, prev_st.get(tt - 2))
            v_o = p.inc("dv")
            p.on("vector", lambda e, tt=tt, bi=bi: e.tensor_add(
                out=ob[bi][:], in0=xq[bi][:], in1=branch_tok[:, tt * D:(tt + 1) * D])
                .then_inc(p.h("dv"), 1))
            prev_add[tt] = v_o
            p.wait("sync", "dv", v_o)
            v_st = p.inc(sos, 16)
            p.on("sync", lambda e, tt=tt, bi=bi, sos=sos: e.dma_start(
                out=out[tt * 128:(tt + 1) * 128, :], in_=ob[bi][:])
                .then_inc(p.h(sos), 16))
            prev_st[tt] = v_st
        p.on("sync", lambda e: e.wait_ge(p.h("os0"), p.cnt["os0"]))
        p.on("sync", lambda e: e.wait_ge(p.h("os1"), p.cnt["os1"]))

    return _finish(nc, p, ctx)


def _host_inputs(core, x):
    b, half = core // 2, core % 2
    if half == 1:
        xkv = np.ascontiguousarray(x[b])
    else:
        xkv = np.concatenate([np.zeros((1024, D), np.float32),
                              np.ascontiguousarray(x[b, 0:1024])], axis=0)
    if half == 1:
        pos = np.arange(TKV, dtype=np.float64)
    else:
        pos = np.maximum(np.arange(TKV, dtype=np.float64) - 1024.0, 0.0)
    inv = 1.0 / (10000.0 ** (np.arange(64, dtype=np.float64) / 64.0))
    ang = pos[None, :] * inv[:, None]
    cos = np.cos(ang)
    sin = np.sin(ang)
    ropev = np.zeros((128, 2 * TKV), dtype=BF)
    ropev[0:64, 0:TKV] = cos.astype(BF)
    ropev[64:128, 0:TKV] = cos.astype(BF)
    ropev[0:64, TKV:] = sin.astype(BF)
    ropev[64:128, TKV:] = sin.astype(BF)
    selv = np.zeros((128, TKV), dtype=BF)
    for j in range(16):
        real = (half == 1) or (j >= 8)
        selv[:, j * 128:(j + 1) * 128] = np.float32(4.0 if real else 0.0)
    return {"x_kv": xkv, "rope": ropev, "sel": selv}


def _attn_layouts(o_codes, o_scales):
    oc = np.ascontiguousarray(
        o_codes.reshape(32, 128, 32, 128).transpose(2, 1, 0, 3).reshape(4096, 4096))
    os2 = np.ascontiguousarray(
        o_scales.reshape(32, 32, 128).transpose(1, 0, 2).reshape(32, 4096))
    return oc, os2


def _ffn_layouts(Wg, Wu, Wd):
    Wu2 = np.ascontiguousarray(
        Wu.reshape(86, 128, 32, 128).transpose(0, 3, 2, 1).reshape(11008, 4096))
    Wg2 = np.ascontiguousarray(
        Wg.reshape(86, 128, 32, 128).transpose(0, 3, 2, 1).reshape(11008, 4096))
    Wd2 = np.ascontiguousarray(
        Wd.reshape(32, 128, 86, 128).transpose(0, 3, 2, 1).reshape(4096, 11008))
    return Wg2, Wu2, Wd2


def kernel(L0_attn_x, L0_attn_q_codes, L0_attn_k_codes, L0_attn_v_codes,
           L0_attn_o_codes, L0_attn_q_scales, L0_attn_k_scales, L0_attn_v_scales,
           L0_attn_o_scales, L0_ffn_W_gate, L0_ffn_W_up, L0_ffn_W_down):
    x = np.asarray(L0_attn_x, dtype=np.float32)
    shared = {
        "q_codes": np.ascontiguousarray(L0_attn_q_codes, dtype=np.int32),
        "k_codes": np.ascontiguousarray(L0_attn_k_codes, dtype=np.int32),
        "v_codes": np.ascontiguousarray(L0_attn_v_codes, dtype=np.int32),
        "o_codes": np.ascontiguousarray(L0_attn_o_codes, dtype=np.int32),
        "q_scales": np.ascontiguousarray(L0_attn_q_scales, dtype=np.float32),
        "k_scales": np.ascontiguousarray(L0_attn_k_scales, dtype=np.float32),
        "v_scales": np.ascontiguousarray(L0_attn_v_scales, dtype=np.float32),
        "o_scales": np.ascontiguousarray(L0_attn_o_scales, dtype=np.float32),
    }
    wg2, wu2, wd2 = _ffn_layouts(np.asarray(L0_ffn_W_gate, dtype=np.float32),
                                 np.asarray(L0_ffn_W_up, dtype=np.float32),
                                 np.asarray(L0_ffn_W_down, dtype=np.float32))
    oc2, os2 = _attn_layouts(shared["o_codes"], shared["o_scales"])
    shared["o_codes"] = oc2
    shared["o_scales"] = os2
    shared["w_gate"] = wg2
    shared["w_up"] = wu2
    shared["w_down"] = wd2
    nc = build_program()
    in_maps = []
    for core in range(8):
        m = dict(shared)
        m.update(_host_inputs(core, x))
        in_maps.append(m)
    res = run_bass_kernel_spmd(nc, in_maps, core_ids=list(range(8)))
    full = np.empty((4, 2048, D), dtype=np.float32)
    for b in range(4):
        full[b, 0:1024] = res.results[2 * b]["out"]
        full[b, 1024:2048] = res.results[2 * b + 1]["out"]
    return full


# version 14
# speedup vs baseline: 1.1489x; 1.1489x over previous
import math
import contextlib
import numpy as np
import ml_dtypes
import concourse.bass as bass
import concourse.mybir as mybir
from concourse.bass_utils import run_bass_kernel_spmd

F32 = mybir.dt.float32
BF16 = mybir.dt.bfloat16
F16 = mybir.dt.float16
I32 = mybir.dt.int32
AF = mybir.ActivationFunctionType
ALU = mybir.AluOpType
AX = mybir.AxisListType
BF = ml_dtypes.bfloat16

D = 4096
KT = 32
TQ = 1024
TKV = 2048
H = 32
KVH = 8
DFF = 11008
FH_SPLITS = [22, 22, 21, 21]
SCALE = 0.25 / math.sqrt(128.0)
EPS = 1e-5
ENGS = ("sync", "gpsimd", "vector", "scalar", "tensor")
J_FULL = {0: 8, 1: 12}
J_ALL = {0: 12, 1: 16}


class Prog:

    def __init__(self, nc, ctx):
        self.nc = nc
        self._ctx = ctx
        self.q = {e: [] for e in ENGS}
        self.sems = {}
        self.cnt = {}
        for name in ("pe", "dv", "ac", "pl"):
            self.sems[name] = ctx.enter_context(nc.semaphore(name))
            self.cnt[name] = 0

    def dsem(self, name):
        if name not in self.sems:
            self.sems[name] = self._ctx.enter_context(self.nc.semaphore(name))
            self.cnt[name] = 0
        return name

    def on(self, eng, fn):
        self.q[eng].append(fn)

    def wait(self, eng, sem, val):
        if val is None or val <= 0:
            return
        h = self.sems[sem]
        self.q[eng].append(lambda e, h=h, v=val: e.wait_ge(h, v))

    def inc(self, sem, k=1):
        self.cnt[sem] += k
        return self.cnt[sem]

    def h(self, name):
        return self.sems[name]

    def barrier(self):
        snap = dict(self.cnt)
        for eng in ENGS:
            for s, v in snap.items():
                self.wait(eng, s, v)


def _finish(nc, p, ctx):
    with nc.Block() as block:
        @block.sync
        def _(e):
            for fn in p.q["sync"]:
                fn(e)

        @block.gpsimd
        def _(e):
            for fn in p.q["gpsimd"]:
                fn(e)

        @block.vector
        def _(e):
            for fn in p.q["vector"]:
                fn(e)

        @block.scalar
        def _(e):
            for fn in p.q["scalar"]:
                fn(e)

        @block.tensor
        def _(e):
            for fn in p.q["tensor"]:
                fn(e)

    ctx.close()
    return nc


def build_program(dbg=None, max_phase=7):
    dbg = set(dbg or [])
    nc = bass.Bass("TRN2", target_bir_lowering=False)
    ctx = contextlib.ExitStack()

    x_kv = nc.dram_tensor("x_kv", [TKV, D], F32, kind="ExternalInput").ap()
    q_codes = nc.dram_tensor("q_codes", [D, D], I32, kind="ExternalInput").ap()
    k_codes = nc.dram_tensor("k_codes", [D, 1024], I32, kind="ExternalInput").ap()
    v_codes = nc.dram_tensor("v_codes", [D, 1024], I32, kind="ExternalInput").ap()
    o_codes = nc.dram_tensor("o_codes", [D, D], I32, kind="ExternalInput").ap()
    q_scales = nc.dram_tensor("q_scales", [KT, D], F32, kind="ExternalInput").ap()
    k_scales = nc.dram_tensor("k_scales", [KT, 1024], F32, kind="ExternalInput").ap()
    v_scales = nc.dram_tensor("v_scales", [KT, 1024], F32, kind="ExternalInput").ap()
    o_scales = nc.dram_tensor("o_scales", [KT, D], F32, kind="ExternalInput").ap()
    w_gate = nc.dram_tensor("w_gate", [DFF, D], F32, kind="ExternalInput").ap()
    w_up = nc.dram_tensor("w_up", [DFF, D], F32, kind="ExternalInput").ap()
    w_down = nc.dram_tensor("w_down", [D, DFF], F32, kind="ExternalInput").ap()
    rope = nc.dram_tensor("rope", [128, 2 * TKV], BF16, kind="ExternalInput").ap()
    sel_in = nc.dram_tensor("sel", [128, TKV], BF16, kind="ExternalInput").ap()
    out = nc.dram_tensor("out", [TQ, D], F32, kind="ExternalOutput").ap()

    ident_c = nc.inline_tensor(np.eye(128, dtype=BF), name="ident_c").ap()
    mm_np = np.where(np.arange(896)[None, :] >= (np.arange(128)[:, None] + 384),
                     np.float16(0.0), np.float16(-30000.0))
    mask_c = nc.inline_tensor(mm_np, name="mask_c").ap()
    ones_c = nc.inline_tensor(np.ones((1, 128), dtype=BF), name="ones_c").ap()
    rot_np = np.zeros((128, 128), dtype=np.float32)
    for _m in range(64):
        rot_np[64 + _m, _m] = -1.0
        rot_np[_m, 64 + _m] = 1.0
    rot_c = nc.inline_tensor(rot_np.astype(BF), name="rot_c").ap()
    eps_c = nc.inline_tensor(np.full((128, 1), EPS, dtype=np.float32), name="eps_c").ap()

    qT_kind = "ExternalOutput" if "qT" in dbg else "Internal"
    yT_kind = "ExternalOutput" if "yT" in dbg else "Internal"
    qT_dram = nc.dram_tensor("qT_scr", [128, H * TQ], BF16, kind=qT_kind).ap()
    yT_dram = nc.dram_tensor("yT_scr", [D, TQ], F32, kind=yT_kind).ap()

    p = Prog(nc, ctx)

    ident = ctx.enter_context(nc.sbuf_tensor("ident", [128, 128], BF16, side="left"))
    onesb = ctx.enter_context(nc.sbuf_tensor("onesb", [1, 128], BF16, side="left"))
    epsb = ctx.enter_context(nc.sbuf_tensor("epsb", [128, 1], F32, side="left"))
    ropeb = ctx.enter_context(nc.sbuf_tensor("ropeb", [128, 2 * TKV], BF16, side="left"))
    rotb = ctx.enter_context(nc.sbuf_tensor("rotb", [128, 128], BF16, side="left"))

    def dbg_dump_sbuf(name, sb, shape, dtype):
        if name not in dbg:
            return
        t = nc.dram_tensor("dbg_" + name, list(shape), dtype, kind="ExternalOutput").ap()
        p.dsem("dbg")
        snap = dict(p.cnt)
        for s, vv in snap.items():
            if s != "dbg":
                p.wait("sync", s, vv)
        p.inc("dbg", 16)
        p.on("sync", lambda e, sb=sb, t=t: e.dma_start(out=t[:, :], in_=sb[:])
             .then_inc(p.h("dbg"), 16))

    p.dsem("cst")
    for dst, src in ((ident, ident_c), (onesb, ones_c), (epsb, eps_c), (ropeb, rope),
                     (rotb, rot_c)):
        p.inc("cst", 16)
        p.on("sync", lambda e, d=dst, s=src: e.dma_start(out=d[:], in_=s[:, :])
             .then_inc(p.h("cst"), 16))
    p.barrier()

    def emit_ln_tile(bufs, row0, dst_view3, trow, ps_T, prev_xn, prev_pe, prev_xn2=0):
        xb0, xb1, stats, mv, srt, rstd, negmr, xnb = bufs
        xb = xb0 if trow % 2 == 0 else xb1
        p.wait("sync", "dv", prev_xn2)
        sx_s = p.dsem(f"lnx{trow % 2}")
        v_sx = p.inc(sx_s, 16)
        p.on("sync", lambda e: e.dma_start(out=xb[:], in_=x_kv[row0:row0 + 128, :])
             .then_inc(p.h(sx_s), 16))
        p.wait("vector", sx_s, v_sx)
        for c in range(8):
            if c < 7:
                p.on("vector", lambda e, c=c: e.bn_stats(out=stats[:, c * 6:(c + 1) * 6],
                                                         in_=xb[:, c * 512:(c + 1) * 512]))
            else:
                v_bs = p.inc("dv")
                p.on("vector", lambda e, c=c: e.bn_stats(out=stats[:, c * 6:(c + 1) * 6],
                                                         in_=xb[:, c * 512:(c + 1) * 512])
                     .then_inc(p.h("dv"), 1))
        p.wait("vector", "dv", v_bs)
        v_mv = p.inc("dv")
        p.on("vector", lambda e: e.bn_aggr(out=mv[:, 0:2], in_=stats[:, 0:48])
             .then_inc(p.h("dv"), 1))
        p.wait("scalar", "dv", v_mv)
        a_srt = p.inc("ac")
        p.on("scalar", lambda e: e.activation(out=srt[:, 0:1], in_=mv[:, 1:2],
                                              func=AF.Sqrt, bias=epsb[:, 0:1])
             .then_inc(p.h("ac"), 1))
        p.wait("vector", "ac", a_srt)
        v_r = p.inc("dv")
        p.on("vector", lambda e: e.reciprocal(out=rstd[:, 0:1], in_=srt[:, 0:1])
             .then_inc(p.h("dv"), 1))
        p.wait("vector", "dv", v_r)
        v_n = p.inc("dv")
        p.on("vector", lambda e: e.tensor_scalar(
            out=negmr[:, 0:1], in0=mv[:, 0:1], scalar1=rstd[:, 0:1], scalar2=-1.0,
            op0=ALU.mult, op1=ALU.mult).then_inc(p.h("dv"), 1))
        p.wait("vector", "dv", v_n)
        p.wait("vector", "pe", prev_pe)
        v_xn = p.inc("dv")
        p.on("vector", lambda e: e.tensor_scalar(
            out=xnb[:], in0=xb[:], scalar1=rstd[:, 0:1], scalar2=negmr[:, 0:1],
            op0=ALU.mult, op1=ALU.add).then_inc(p.h("dv"), 1))
        p.wait("tensor", "dv", v_xn)
        last_pe = None
        for b in range(4):
            for i in range(8):
                k = b * 8 + i
                def t_fn(e, k=k, b=b, i=i):
                    return e.transpose(ps_T[b][:, i * 128:(i + 1) * 128],
                                       xnb[:, k * 128:(k + 1) * 128], ident[:])
                if i < 7:
                    p.on("tensor", t_fn)
                else:
                    last_pe = p.inc("pe")
                    p.on("tensor", lambda e, f=t_fn: f(e).then_inc(p.h("pe"), 1))
            p.wait("vector", "pe", last_pe)
            v_ev = p.inc("dv")
            def ev_fn(e, b=b, trow=trow):
                src = ps_T[b][:, 0:1024].rearrange("p (a n) -> p a n", a=8)
                d = dst_view3[:, b * 8:(b + 1) * 8, trow * 128:(trow + 1) * 128]
                return e.tensor_copy(out=d, in_=src).then_inc(p.h("dv"), 1)
            p.on("vector", ev_fn)
        return v_xn, last_pe, p.cnt["dv"]

    def ln_bufs(c, pfx):
        return (
            c.enter_context(nc.sbuf_tensor(pfx + "xb", [128, D], F32, side="right")),
            c.enter_context(nc.sbuf_tensor(pfx + "xc", [128, D], F32, side="right")),
            c.enter_context(nc.sbuf_tensor(pfx + "st", [128, 48], F32, side="right")),
            c.enter_context(nc.sbuf_tensor(pfx + "mv", [128, 2], F32, side="right")),
            c.enter_context(nc.sbuf_tensor(pfx + "sq", [128, 1], F32, side="right")),
            c.enter_context(nc.sbuf_tensor(pfx + "rs", [128, 1], F32, side="right")),
            c.enter_context(nc.sbuf_tensor(pfx + "nm", [128, 1], F32, side="right")),
            c.enter_context(nc.sbuf_tensor(pfx + "xn", [128, D], BF16, side="right")),
        )

    xnT_q_cm = nc.sbuf_tensor("xnT_q", [128, KT * TQ], BF16, side="right")
    xnT_q = xnT_q_cm.__enter__()
    xnT_q_v3 = xnT_q.ap().rearrange("p (a r) -> p a r", a=KT)

    with contextlib.ExitStack() as lctx:
        bufs = ln_bufs(lctx, "l_")
        ps_T = [lctx.enter_context(nc.psum_tensor(f"psT{b}", [128, 1024], BF16))
                for b in range(4)]
        prev_xn, prev_pe, prev_xn2 = 0, 0, 0
        for tt in range(8):
            p2x = prev_xn
            prev_xn, prev_pe, _ = emit_ln_tile(bufs, 1024 + tt * 128, xnT_q_v3, tt,
                                               ps_T, prev_xn, prev_pe, prev_xn2)
            prev_xn2 = p2x
    p.barrier()
    dbg_dump_sbuf("xnT_q", xnT_q, (128, KT * TQ), BF16)
    if max_phase <= 1:
        return _finish(nc, p, ctx)

    with contextlib.ExitStack() as fctx:
        wT_u = [fctx.enter_context(nc.sbuf_tensor(f"wTu{i}", [128, D], BF16, side="right"))
                for i in range(2)]
        wT_g = [fctx.enter_context(nc.sbuf_tensor(f"wTg{i}", [128, D], BF16, side="right"))
                for i in range(2)]
        h_sb = fctx.enter_context(nc.sbuf_tensor("h_sb", [128, 22 * 1024], BF16, side="right"))
        g_sb = fctx.enter_context(nc.sbuf_tensor("g_sb", [128, 2048], BF16, side="right"))
        wdT = [fctx.enter_context(nc.sbuf_tensor(f"wdT{i}", [128, 2816], BF16, side="right"))
               for i in range(2)]
        yprev = [fctx.enter_context(nc.sbuf_tensor(f"ypv{i}", [128, TQ], F32, side="right"))
                 for i in range(2)]
        yrow = [fctx.enter_context(nc.sbuf_tensor(f"yrw{i}", [128, TQ], F32, side="right"))
                for i in range(2)]
        ps_mm = [fctx.enter_context(nc.psum_tensor(f"psmm{b}", [128, 512], F32))
                 for b in range(8)]

        ffn_mm = {}
        ffn_h = {}

        f0 = 0
        for fh, nfi in enumerate(FH_SPLITS):
            for fi_loc in range(nfi):
                fi = f0 + fi_loc
                bi = fi % 2
                p.wait("gpsimd", "pe", ffn_mm.get(fi - 2))
                su, sg = p.dsem(f"wu{bi}"), p.dsem(f"wg{bi}")
                g1 = p.inc(su, 16)
                p.on("gpsimd", lambda e, fi=fi, bi=bi, su=su: e.dma_start(
                    out=wT_u[bi][:], in_=w_up[fi * 128:(fi + 1) * 128, :])
                    .then_inc(p.h(su), 16))
                g2 = p.inc(sg, 16)
                p.on("gpsimd", lambda e, fi=fi, bi=bi, sg=sg: e.dma_start(
                    out=wT_g[bi][:], in_=w_gate[fi * 128:(fi + 1) * 128, :])
                    .then_inc(p.h(sg), 16))
                p.wait("tensor", su, g1)
                p.wait("tensor", sg, g2)
                p.wait("tensor", "dv", ffn_h.get(fi - 2))
                pb = 4 * (fi % 2)
                v_mm = None
                for k in range(KT):
                    for half, wT in ((0, wT_u[bi]), (1, wT_g[bi])):
                        for tc in range(2):
                            bank = ps_mm[pb + half * 2 + tc]
                            def mm_fn(e, wT=wT, k=k, bank=bank, tc=tc):
                                return e.matmul(
                                    bank[:], wT[:, k * 128:(k + 1) * 128],
                                    xnT_q[:, k * TQ + tc * 512:k * TQ + (tc + 1) * 512],
                                    start=(k == 0), stop=(k == KT - 1))
                            if k == KT - 1 and half == 1 and tc == 1:
                                v_mm = p.inc("pe")
                                p.on("tensor", lambda e, f=mm_fn: f(e).then_inc(p.h("pe"), 1))
                            else:
                                p.on("tensor", mm_fn)
                ffn_mm[fi] = v_mm
                p.wait("scalar", "pe", v_mm)
                p.wait("scalar", "dv", ffn_h.get(fi - 2))
                a_s = None
                for tc in range(2):
                    a_s = p.inc("ac")
                    p.on("scalar", lambda e, tc=tc, pb=pb, fi_loc=fi_loc: e.activation(
                        out=g_sb[:, (fi_loc % 2) * 1024 + tc * 512:
                                 (fi_loc % 2) * 1024 + (tc + 1) * 512],
                        in_=ps_mm[pb + 2 + tc][:],
                        func=AF.Silu).then_inc(p.h("ac"), 1))
                p.wait("vector", "ac", a_s)
                v_h = None
                for tc in range(2):
                    v_h = p.inc("dv")
                    p.on("vector", lambda e, tc=tc, fi_loc=fi_loc, pb=pb: e.tensor_mul(
                        out=h_sb[:, fi_loc * 1024 + tc * 512:fi_loc * 1024 + (tc + 1) * 512],
                        in0=ps_mm[pb + tc][:],
                        in1=g_sb[:, (fi_loc % 2) * 1024 + tc * 512:
                                 (fi_loc % 2) * 1024 + (tc + 1) * 512])
                        .then_inc(p.h("dv"), 1))
                ffn_h[fi] = v_h
            p.wait("tensor", "dv", ffn_h[f0 + nfi - 1])

            dn_mm = {}
            dn_y = {}
            dn_st = {}
            for dm in range(KT):
                bi = dm % 2
                p.wait("gpsimd", "pe", dn_mm.get(dm - 2))
                sd = p.dsem(f"wd{bi}")
                gD = p.inc(sd, 16)
                p.on("gpsimd", lambda e, dm=dm, bi=bi, f0=f0, nfi=nfi, sd=sd: e.dma_start(
                    out=wdT[bi][:, 0:nfi * 128],
                    in_=w_down[dm * 128:(dm + 1) * 128, f0 * 128:(f0 + nfi) * 128])
                    .then_inc(p.h(sd), 16))
                sxp = None
                sy = p.dsem(f"yp{bi}")
                if fh > 0:
                    p.wait("sync", "dv", dn_y.get(dm - 2))
                    sxp = p.inc(sy, 16)
                    p.on("sync", lambda e, dm=dm, bi=bi, sy=sy: e.dma_start(
                        out=yprev[bi][:], in_=yT_dram[dm * 128:(dm + 1) * 128, :])
                        .then_inc(p.h(sy), 16))
                p.wait("tensor", sd, gD)
                p.wait("tensor", "dv", dn_y.get(dm - 2))
                v_mm = None
                for tc in range(2):
                    bank = ps_mm[2 * bi + tc]
                    for jf in range(nfi):
                        def mm_fn(e, jf=jf, bank=bank, tc=tc, nfi=nfi, bi=bi):
                            return e.matmul(
                                bank[:], wdT[bi][:, jf * 128:(jf + 1) * 128],
                                h_sb[:, jf * 1024 + tc * 512:jf * 1024 + (tc + 1) * 512],
                                start=(jf == 0), stop=(jf == nfi - 1))
                        if jf == nfi - 1 and tc == 1:
                            v_mm = p.inc("pe")
                            p.on("tensor", lambda e, f=mm_fn: f(e).then_inc(p.h("pe"), 1))
                        else:
                            p.on("tensor", mm_fn)
                dn_mm[dm] = v_mm
                p.wait("vector", "pe", v_mm)
                if sxp:
                    p.wait("vector", sy, sxp)
                syr = p.dsem(f"yr{bi}")
                p.wait("vector", syr, dn_st.get(dm - 2))
                v_y = None
                for tc in range(2):
                    bank = ps_mm[2 * bi + tc]
                    v_y = p.inc("dv")
                    if fh == 0:
                        p.on("vector", lambda e, bank=bank, tc=tc, bi=bi: e.tensor_copy(
                            out=yrow[bi][:, tc * 512:(tc + 1) * 512], in_=bank[:])
                            .then_inc(p.h("dv"), 1))
                    else:
                        p.on("vector", lambda e, bank=bank, tc=tc, bi=bi: e.tensor_add(
                            out=yrow[bi][:, tc * 512:(tc + 1) * 512], in0=bank[:],
                            in1=yprev[bi][:, tc * 512:(tc + 1) * 512])
                            .then_inc(p.h("dv"), 1))
                dn_y[dm] = v_y
                p.wait("sync", "dv", v_y)
                v_st = p.inc(syr, 16)
                p.on("sync", lambda e, dm=dm, bi=bi, syr=syr: e.dma_start(
                    out=yT_dram[dm * 128:(dm + 1) * 128, :], in_=yrow[bi][:])
                    .then_inc(p.h(syr), 16))
                dn_st[dm] = v_st
            p.barrier()
            f0 += nfi
    p.barrier()
    if max_phase <= 2:
        return _finish(nc, p, ctx)

    with contextlib.ExitStack() as qctx:
        cq = [qctx.enter_context(nc.sbuf_tensor(f"cq{i}", [128, 512], BF16, side="right"))
              for i in range(4)]
        bq = [qctx.enter_context(nc.sbuf_tensor(f"bq{i}", [128, 512], BF16, side="right"))
              for i in range(4)]
        wq = [qctx.enter_context(nc.sbuf_tensor(f"wq{i}", [128, 512], BF16, side="right"))
              for i in range(4)]
        qtmp = qctx.enter_context(nc.sbuf_tensor("qtmp", [128, 512], BF16, side="right"))
        m1 = qctx.enter_context(nc.sbuf_tensor("m1", [128, 512], BF16, side="right"))
        m2 = qctx.enter_context(nc.sbuf_tensor("m2", [128, 512], BF16, side="right"))
        qro = [qctx.enter_context(nc.sbuf_tensor(f"qro{i}", [128, 512], BF16, side="right"))
               for i in range(2)]
        ps_q = [qctx.enter_context(nc.psum_tensor(f"psq{b}", [128, 512], F32))
                for b in range(8)]

        q_dq = {}
        q_mm = {}
        q_ro = {}
        q_rodv = {}
        q_bankfree = {}
        for gq in range(8):
            v_mm_last = None
            for k in range(KT):
                bi = k % 4
                p.wait("gpsimd", "dv", q_dq.get((gq * KT + k) - 4))
                sc_, sb_ = p.dsem(f"cq{bi}"), p.dsem(f"bq{bi}")
                gc = p.inc(sc_, 16)
                p.on("gpsimd", lambda e, gq=gq, k=k, bi=bi, sc_=sc_: e.dma_start(
                    out=cq[bi][:], in_=q_codes[k * 128:(k + 1) * 128,
                                              gq * 512:(gq + 1) * 512])
                    .then_inc(p.h(sc_), 16))
                gs = p.inc(sb_, 16)
                p.on("gpsimd", lambda e, gq=gq, k=k, bi=bi, sb_=sb_: e.dma_start(
                    out=bq[bi][:], in_=q_scales[k:k + 1, gq * 512:(gq + 1) * 512]
                    .broadcast_to([128, 512])).then_inc(p.h(sb_), 16))
                p.wait("vector", sc_, gc)
                p.wait("vector", sb_, gs)
                p.wait("vector", "pe", q_mm.get((gq * KT + k) - 4))
                v_dq = p.inc("dv")
                p.on("vector", lambda e, bi=bi: e.scalar_tensor_tensor(
                    out=wq[bi][:], in0=cq[bi][:], scalar=-8.0, in1=bq[bi][:],
                    op0=ALU.add, op1=ALU.mult).then_inc(p.h("dv"), 1))
                q_dq[gq * KT + k] = v_dq
                p.wait("tensor", "dv", v_dq)
                for h4 in range(4):
                    for tc in range(2):
                        bank = ps_q[h4 * 2 + tc]
                        if k == 0:
                            p.wait("tensor", "dv", q_bankfree.get((gq - 1, h4, tc)))
                        def mm_fn(e, bi=bi, h4=h4, tc=tc, k=k, bank=bank):
                            return e.matmul(
                                bank[:], wq[bi][:, h4 * 128:(h4 + 1) * 128],
                                xnT_q[:, k * TQ + tc * 512:k * TQ + (tc + 1) * 512],
                                start=(k == 0), stop=(k == KT - 1))
                        if h4 == 3 and tc == 1:
                            v_mm_last = p.inc("pe")
                            p.on("tensor", lambda e, f=mm_fn: f(e).then_inc(p.h("pe"), 1))
                        else:
                            p.on("tensor", mm_fn)
                q_mm[gq * KT + k] = v_mm_last
            st_hist = []
            last_rot = 0
            for h4 in range(4):
                h = gq * 4 + h4
                for tc in range(2):
                    bank = ps_q[h4 * 2 + tc]
                    p.wait("scalar", "pe", v_mm_last)
                    p.wait("scalar", "pe", last_rot)
                    p.wait("scalar", "dv", p.cnt["dv"])
                    a_c = p.inc("ac")
                    p.on("scalar", lambda e, bank=bank: e.copy(out=qtmp[:], in_=bank[:])
                         .then_inc(p.h("ac"), 1))
                    p.wait("tensor", "ac", a_c)
                    v_rot = p.inc("pe")
                    last_rot = v_rot
                    p.on("tensor", lambda e, bank=bank: e.matmul(
                        bank[:], rotb[:], qtmp[:], start=True, stop=True)
                        .then_inc(p.h("pe"), 1))
                    p.wait("vector", "ac", a_c)
                    p.wait("vector", "pe", v_rot)
                    col = 1024 + tc * 512
                    bi2 = len(st_hist) % 2
                    sq_ = p.dsem(f"qs{bi2}")
                    if len(st_hist) >= 2:
                        p.wait("vector", sq_, st_hist[-2])
                    p.on("vector", lambda e, col=col: e.tensor_mul(
                        out=m1[:], in0=qtmp[:], in1=ropeb[:, col:col + 512]))
                    v_m2 = p.inc("dv")
                    p.on("vector", lambda e, col=col, bank=bank: e.tensor_mul(
                        out=m2[:], in0=bank[:], in1=ropeb[:, 2048 + col:2048 + col + 512])
                        .then_inc(p.h("dv"), 1))
                    p.wait("vector", "dv", v_m2)
                    q_bankfree[(gq, h4, tc)] = v_m2
                    v_ro = p.inc("dv")
                    p.on("vector", lambda e, bi2=bi2: e.tensor_add(
                        out=qro[bi2][:], in0=m1[:], in1=m2[:])
                        .then_inc(p.h("dv"), 1))
                    p.wait("sync", "dv", v_ro)
                    v_st = p.inc(sq_, 16)
                    p.on("sync", lambda e, h=h, tc=tc, bi2=bi2, sq_=sq_: e.dma_start(
                        out=qT_dram[:, h * 1024 + tc * 512:h * 1024 + (tc + 1) * 512],
                        in_=qro[bi2][:]).then_inc(p.h(sq_), 16))
                    st_hist.append(v_st)
            q_ro[gq] = p.cnt["ac"]
            q_rodv[gq] = p.cnt["dv"]
    p.barrier()
    if max_phase <= 3:
        return _finish(nc, p, ctx)
    xnT_q_cm.__exit__(None, None, None)

    kv_stack = contextlib.ExitStack()
    kT = kv_stack.enter_context(nc.sbuf_tensor("kT", [128, KVH * TKV], BF16, side="left"))
    vsb = kv_stack.enter_context(nc.sbuf_tensor("vsb", [128, KVH * 16 * 128], BF16,
                                                side="left"))
    maskb = kv_stack.enter_context(nc.sbuf_tensor("maskb", [128, 896], F16, side="left"))
    selb = kv_stack.enter_context(nc.sbuf_tensor("selb", [128, TKV], BF16, side="left"))
    p.dsem("cst2")
    for dst, src in ((maskb, mask_c), (selb, sel_in)):
        p.inc("cst2", 16)
        p.on("sync", lambda e, d=dst, s=src: e.dma_start(out=d[:], in_=s[:, :])
             .then_inc(p.h("cst2"), 16))
    vsb_v3 = vsb.ap().rearrange("p (a r) -> p a r", a=KVH)

    with contextlib.ExitStack() as kctx:
        xnT_cc = kctx.enter_context(nc.sbuf_tensor("xnT_cc", [128, KT * 512], BF16,
                                                   side="right"))
        xnT_cc_v3 = xnT_cc.ap().rearrange("p (a r) -> p a r", a=KT)
        ck = [kctx.enter_context(nc.sbuf_tensor(f"ck{i}", [128, 1024], BF16, side="right"))
              for i in range(4)]
        bk = [kctx.enter_context(nc.sbuf_tensor(f"bk{i}", [128, 1024], BF16, side="right"))
              for i in range(4)]
        wk = [kctx.enter_context(nc.sbuf_tensor(f"wk{i}", [128, 1024], BF16, side="right"))
              for i in range(4)]
        ktmp = kctx.enter_context(nc.sbuf_tensor("ktmp", [128, 512], BF16, side="right"))
        km1 = kctx.enter_context(nc.sbuf_tensor("km1", [128, 512], BF16, side="right"))
        km2 = kctx.enter_context(nc.sbuf_tensor("km2", [128, 512], BF16, side="right"))
        lnb = ln_bufs(kctx, "k_")

        kv_dq = {}
        kv_mm = {}
        dq_i = 0
        for cc in range(4):
            if cc < 2:
                with contextlib.ExitStack() as tctx:
                    ps_T = [tctx.enter_context(nc.psum_tensor(f"psKT{cc}{b}", [128, 1024],
                                                              BF16)) for b in range(4)]
                    prev_xn, prev_pe, prev_xn2 = 0, 0, 0
                    for t4 in range(4):
                        p2x = prev_xn
                        prev_xn, prev_pe, _ = emit_ln_tile(
                            lnb, cc * 512 + t4 * 128, xnT_cc_v3, t4, ps_T,
                            prev_xn, prev_pe, prev_xn2)
                        prev_xn2 = p2x
                p.barrier()

                def xn_slice(k, cc=cc):
                    return xnT_cc[:, k * 512:(k + 1) * 512]
                def xn_block(k, tb, cc=cc):
                    return xnT_cc[:, k * 512 + tb * 128:k * 512 + (tb + 1) * 128]
            else:
                def xn_slice(k, cc=cc):
                    return xnT_q[:, k * TQ + (cc - 2) * 512:k * TQ + (cc - 2) * 512 + 512]
                def xn_block(k, tb, cc=cc):
                    b0 = k * TQ + (cc - 2) * 512 + tb * 128
                    return xnT_q[:, b0:b0 + 128]
            if cc >= 2:
                with contextlib.ExitStack() as tctx:
                    ps_T = [tctx.enter_context(nc.psum_tensor(f"psKT{cc}{b}", [128, 1024],
                                                              BF16)) for b in range(4)]
                    prev_xn, prev_pe, prev_xn2 = 0, 0, 0
                    for t4 in range(4):
                        p2x = prev_xn
                        prev_xn, prev_pe, _ = emit_ln_tile(
                            lnb, cc * 512 + t4 * 128, xnT_cc_v3, t4, ps_T,
                            prev_xn, prev_pe, prev_xn2)
                        prev_xn2 = p2x
                p.barrier()

                def xn_slice(k, cc=cc):
                    return xnT_cc[:, k * 512:(k + 1) * 512]
                def xn_block(k, tb, cc=cc):
                    return xnT_cc[:, k * 512 + tb * 128:k * 512 + (tb + 1) * 128]

            with contextlib.ExitStack() as pctx:
                ps_kv = [pctx.enter_context(nc.psum_tensor(f"pskv{cc}{b}", [128, 512], F32))
                         for b in range(8)]
                v_mm_last = None
                for k in range(KT):
                    bi = k % 4
                    p.wait("gpsimd", "dv", kv_dq.get(dq_i - 4))
                    sc_, sb_ = p.dsem(f"ck{bi}"), p.dsem(f"bk{bi}")
                    gc = p.inc(sc_, 16)
                    p.on("gpsimd", lambda e, k=k, bi=bi, sc_=sc_: e.dma_start(
                        out=ck[bi][:], in_=k_codes[k * 128:(k + 1) * 128, :])
                        .then_inc(p.h(sc_), 16))
                    gs = p.inc(sb_, 16)
                    p.on("gpsimd", lambda e, k=k, bi=bi, sb_=sb_: e.dma_start(
                        out=bk[bi][:], in_=k_scales[k:k + 1, :].broadcast_to([128, 1024]))
                        .then_inc(p.h(sb_), 16))
                    p.wait("vector", sc_, gc)
                    p.wait("vector", sb_, gs)
                    p.wait("vector", "pe", kv_mm.get(dq_i - 4))
                    v_dq = p.inc("dv")
                    p.on("vector", lambda e, bi=bi: e.scalar_tensor_tensor(
                        out=wk[bi][:], in0=ck[bi][:], scalar=-8.0, in1=bk[bi][:],
                        op0=ALU.add, op1=ALU.mult).then_inc(p.h("dv"), 1))
                    kv_dq[dq_i] = v_dq
                    p.wait("tensor", "dv", v_dq)
                    for g in range(8):
                        def mm_fn(e, bi=bi, g=g, k=k):
                            return e.matmul(ps_kv[g][:], wk[bi][:, g * 128:(g + 1) * 128],
                                            xn_slice(k), start=(k == 0), stop=(k == KT - 1))
                        if g == 7:
                            v_mm_last = p.inc("pe")
                            p.on("tensor", lambda e, f=mm_fn: f(e).then_inc(p.h("pe"), 1))
                        else:
                            p.on("tensor", mm_fn)
                    kv_mm[dq_i] = v_mm_last
                    dq_i += 1
                v_ro_last = None
                last_rot = 0
                for g in range(8):
                    p.wait("scalar", "pe", v_mm_last)
                    p.wait("scalar", "pe", last_rot)
                    if v_ro_last:
                        p.wait("scalar", "dv", v_ro_last)
                    a_c = p.inc("ac")
                    p.on("scalar", lambda e, g=g: e.copy(out=ktmp[:], in_=ps_kv[g][:])
                         .then_inc(p.h("ac"), 1))
                    p.wait("tensor", "ac", a_c)
                    v_rot = p.inc("pe")
                    last_rot = v_rot
                    p.on("tensor", lambda e, g=g: e.matmul(
                        ps_kv[g][:], rotb[:], ktmp[:], start=True, stop=True)
                        .then_inc(p.h("pe"), 1))
                    p.wait("vector", "ac", a_c)
                    p.wait("vector", "pe", v_rot)
                    col = cc * 512
                    dst0 = g * TKV + cc * 512
                    p.on("vector", lambda e, col=col: e.tensor_mul(
                        out=km1[:], in0=ktmp[:], in1=ropeb[:, col:col + 512]))
                    v_m2 = p.inc("dv")
                    p.on("vector", lambda e, col=col, g=g: e.tensor_mul(
                        out=km2[:], in0=ps_kv[g][:], in1=ropeb[:, 2048 + col:2048 + col + 512])
                        .then_inc(p.h("dv"), 1))
                    p.wait("vector", "dv", v_m2)
                    v_ro_last = p.inc("dv")
                    p.on("vector", lambda e, dst0=dst0: e.tensor_add(
                        out=kT[:, dst0:dst0 + 512], in0=km1[:], in1=km2[:])
                        .then_inc(p.h("dv"), 1))
                p.wait("tensor", "ac", p.cnt["ac"])
                p.wait("tensor", "dv", p.cnt["dv"])
                for dvs in range(2):
                    v_mm_last = None
                    for k in range(KT):
                        bi = k % 4
                        p.wait("gpsimd", "dv", kv_dq.get(dq_i - 4))
                        sc_, sb_ = p.dsem(f"ck{bi}"), p.dsem(f"bk{bi}")
                        gc = p.inc(sc_, 16)
                        p.on("gpsimd", lambda e, k=k, bi=bi, dvs=dvs, sc_=sc_: e.dma_start(
                            out=ck[bi][:, 0:512],
                            in_=v_codes[k * 128:(k + 1) * 128, dvs * 512:(dvs + 1) * 512])
                            .then_inc(p.h(sc_), 16))
                        gs = p.inc(sb_, 16)
                        p.on("gpsimd", lambda e, k=k, bi=bi, dvs=dvs, sb_=sb_: e.dma_start(
                            out=bk[bi][:, 0:512],
                            in_=v_scales[k:k + 1, dvs * 512:(dvs + 1) * 512]
                            .broadcast_to([128, 512])).then_inc(p.h(sb_), 16))
                        p.wait("vector", sc_, gc)
                        p.wait("vector", sb_, gs)
                        p.wait("vector", "pe", kv_mm.get(dq_i - 4))
                        v_dq = p.inc("dv")
                        p.on("vector", lambda e, bi=bi: e.scalar_tensor_tensor(
                            out=wk[bi][:, 0:512], in0=ck[bi][:, 0:512], scalar=-8.0,
                            in1=bk[bi][:, 0:512], op0=ALU.add, op1=ALU.mult)
                            .then_inc(p.h("dv"), 1))
                        kv_dq[dq_i] = v_dq
                        p.wait("tensor", "dv", v_dq)
                        for tb in range(4):
                            bank = ps_kv[dvs * 4 + tb]
                            def mm_fn(e, bi=bi, tb=tb, k=k, bank=bank):
                                return e.matmul(bank[:], xn_block(k, tb), wk[bi][:, 0:512],
                                                start=(k == 0), stop=(k == KT - 1))
                            if tb == 3:
                                v_mm_last = p.inc("pe")
                                p.on("tensor", lambda e, f=mm_fn: f(e)
                                     .then_inc(p.h("pe"), 1))
                            else:
                                p.on("tensor", mm_fn)
                        kv_mm[dq_i] = v_mm_last
                        dq_i += 1
                    p.wait("vector", "pe", v_mm_last)
                    v_ev = None
                    for tb in range(4):
                        j = cc * 4 + tb
                        bank = ps_kv[dvs * 4 + tb]
                        v_ev = p.inc("dv")
                        def ev_fn(e, bank=bank, j=j, dvs=dvs):
                            src = bank[:].rearrange("p (a n) -> p a n", a=4)
                            d = vsb_v3[:, dvs * 4:(dvs + 1) * 4, j * 128:(j + 1) * 128]
                            return e.tensor_copy(out=d, in_=src).then_inc(p.h("dv"), 1)
                        p.on("vector", ev_fn)
                    p.wait("tensor", "dv", v_ev)
            p.barrier()
    p.barrier()
    dbg_dump_sbuf("kT", kT, (128, KVH * TKV), BF16)
    dbg_dump_sbuf("vsb", vsb, (128, KVH * 16 * 128), BF16)
    if max_phase <= 4:
        kv_stack.close()
        return _finish(nc, p, ctx)

    an_sb = ctx.enter_context(nc.sbuf_tensor("an_sb", [128, H * TQ], BF16, side="right"))

    with contextlib.ExitStack() as actx:
        qh_sb = [actx.enter_context(nc.sbuf_tensor(f"qh{i}", [128, TQ], BF16, side="right"))
                 for i in range(8)]
        s16 = [actx.enter_context(nc.sbuf_tensor(f"s16_{i}", [128, 16 * 512], F16,
                                                 side="right")) for i in range(2)]
        s16m = actx.enter_context(nc.sbuf_tensor("s16m", [128, 512], F16, side="right"))
        rm_all = actx.enter_context(nc.sbuf_tensor("rm_all", [128, 32], BF16, side="right"))
        rm_f = actx.enter_context(nc.sbuf_tensor("rm_f", [128, 1], BF16, side="right"))
        bmax = actx.enter_context(nc.sbuf_tensor("bmax", [1, 1], BF16, side="right"))
        bias_h = [actx.enter_context(nc.sbuf_tensor(f"bh{i}", [128, 1], F32, side="right"))
                  for i in range(8)]
        exp4 = actx.enter_context(nc.sbuf_tensor("exp4", [128, 4 * 512], BF16,
                                                 side="right"))
        recip = actx.enter_context(nc.sbuf_tensor("rcp", [128, 512], F32, side="right"))
        ps_A = [actx.enter_context(nc.psum_tensor(f"psA{i}", [128, 512], F32))
                for i in range(2)]
        ps_sum = [actx.enter_context(nc.psum_tensor(f"psM{i}", [128, 512], F32))
                  for i in range(2)]
        ps_pv = [actx.enter_context(nc.psum_tensor(f"psP{i}", [128, 512], F32))
                 for i in range(2)]

        a_ev = {}
        a_rm = {}
        s16_ev = {}
        s16_rd = {}
        runA = [0]
        b_exp = {}
        b_mm = {}
        runB = [0]
        bias_ac = {}
        qh_done = {}
        qh_lastuse = {}

        def prefetch_qh(g):
            if g >= KVH:
                return
            for h4 in range(4):
                hh = g * 4 + h4
                slot = h4 + 4 * (g % 2)
                sq_ = p.dsem(f"qh{slot}")
                p.wait("sync", "pe", qh_lastuse.get((g - 2, h4)))
                v_q = p.inc(sq_, 16)
                p.on("sync", lambda e, hh=hh, slot=slot, sq_=sq_: e.dma_start(
                    out=qh_sb[slot][:], in_=qT_dram[:, hh * TQ:(hh + 1) * TQ])
                    .then_inc(p.h(sq_), 16))
                qh_done[(g, h4)] = (sq_, v_q)

        def emit_A(g, h4, c):
            slot = h4 + 4 * (g % 2)
            sq_, v_q = qh_done[(g, h4)]
            p.wait("tensor", sq_, v_q)
            for j in range(J_ALL[c]):
                t = runA[0]
                bank = ps_A[t % 2]
                dep = a_ev.get(t - 2)
                if dep:
                    p.wait("tensor", dep[0], dep[1])
                v_pe = p.inc("pe")
                p.on("tensor", lambda e, bank=bank, g=g, j=j, slot=slot, c=c:
                     e.matmul(bank[:],
                              kT[:, g * TKV + j * 128:g * TKV + (j + 1) * 128],
                              qh_sb[slot][:, c * 512:(c + 1) * 512],
                              start=True, stop=True)
                     .then_inc(p.h("pe"), 1))
                qh_lastuse[(g, h4)] = v_pe
                eng = "vector" if (t % 2 == 0) else "scalar"
                p.wait(eng, "pe", v_pe)
                rd = s16_rd.get((c, j))
                if rd:
                    p.wait(eng, rd[0], rd[1])
                if eng == "vector":
                    v_e = p.inc("dv")
                    p.on("vector", lambda e, bank=bank, c=c, j=j: e.tensor_copy(
                        out=s16[c % 2][:, j * 512:(j + 1) * 512], in_=bank[:])
                        .then_inc(p.h("dv"), 1))
                    a_ev[t] = ("dv", v_e)
                    s16_ev[(c, j)] = ("dv", v_e)
                    v_rm = ("dv", p.inc("dv"))
                    p.on("vector", lambda e, c=c, j=j: e.reduce_max(
                        out=rm_all[:, j:j + 1] if c == 0 else rm_all[:, 12 + j:13 + j],
                        in_=s16[c % 2][:, j * 512:(j + 1) * 512], axis=AX.X)
                        .then_inc(p.h("dv"), 1))
                else:
                    a_e = p.inc("ac")
                    p.on("scalar", lambda e, bank=bank, c=c, j=j: e.copy(
                        out=s16[c % 2][:, j * 512:(j + 1) * 512], in_=bank[:])
                        .then_inc(p.h("ac"), 1))
                    a_ev[t] = ("ac", a_e)
                    s16_ev[(c, j)] = ("ac", a_e)
                    p.wait("vector", "ac", a_e)
                    v_rm = ("dv", p.inc("dv"))
                    p.on("vector", lambda e, c=c, j=j: e.reduce_max(
                        out=rm_all[:, j:j + 1] if c == 0 else rm_all[:, 12 + j:13 + j],
                        in_=s16[c % 2][:, j * 512:(j + 1) * 512], axis=AX.X)
                        .then_inc(p.h("dv"), 1))
                a_rm[t] = v_rm
                runA[0] = t + 1

        def emit_bias(g, h4):
            slot = h4 + 4 * (g % 2)
            for tt in range(runA[0] - 28, runA[0]):
                s_, v_ = a_rm[tt]
                p.wait("vector", s_, v_)
            v_rf = p.inc("dv")
            p.on("vector", lambda e: e.reduce_max(
                out=rm_f[:, 0:1], in_=rm_all[:, 0:28], axis=AX.X)
                .then_inc(p.h("dv"), 1))
            p.wait("tensor", "dv", v_rf)
            bankT = ps_A[runA[0] % 2]
            dep = a_ev.get(runA[0] - 2)
            if dep:
                p.wait("tensor", dep[0], dep[1])
            v_pe = p.inc("pe")
            p.on("tensor", lambda e, bankT=bankT: e.transpose(
                bankT.ap().bitcast(BF16)[0:1, 0:128], rm_f[:, 0:1], ident[:])
                .then_inc(p.h("pe"), 1))
            p.wait("vector", "pe", v_pe)
            v_bm = p.inc("dv")
            p.on("vector", lambda e, bankT=bankT: e.reduce_max(
                out=bmax[0:1, 0:1], in_=bankT.ap().bitcast(BF16)[0:1, 0:128],
                axis=AX.X).then_inc(p.h("dv"), 1))
            p.wait("tensor", "dv", v_bm)
            v_pe2 = p.inc("pe")
            p.on("tensor", lambda e, bankT=bankT: e.matmul(
                bankT[:, 0:1], onesb[0:1, :], bmax[0:1, 0:1],
                start=True, stop=True).then_inc(p.h("pe"), 1))
            p.wait("scalar", "pe", v_pe2)
            a_b = p.inc("ac")
            p.on("scalar", lambda e, slot=slot, bankT=bankT: e.activation(
                out=bias_h[slot][:, 0:1], in_=bankT[:, 0:1], func=AF.Identity,
                scale=-SCALE).then_inc(p.h("ac"), 1))
            bias_ac[(g, h4)] = a_b
            a_ev[runA[0] - 2] = ("ac", a_b)

        def emit_B(g, h4, c, ms):
            slot = h4 + 4 * (g % 2)
            n_all = J_ALL[c]
            n_full = J_FULL[c]
            p.wait("scalar", "ac", bias_ac[(g, h4)])
            prev_diag = None
            for j in range(n_all):
                t = runB[0]
                se_slot = t % 4
                diag = j >= n_full
                sev = s16_ev.get((c, j))
                if diag:
                    s_off = 384 - (j - n_full) * 128
                    p.wait("vector", "ac", prev_diag or 0)
                    if sev:
                        p.wait("vector", sev[0], sev[1])
                    v_m = p.inc("dv")
                    p.on("vector", lambda e, c=c, j=j, s_off=s_off: e.tensor_add(
                        out=s16m[:], in0=s16[c % 2][:, j * 512:(j + 1) * 512],
                        in1=maskb[:, s_off:s_off + 512]).then_inc(p.h("dv"), 1))
                    p.wait("scalar", "dv", v_m)
                elif sev:
                    p.wait("scalar", sev[0], sev[1])
                p.wait("scalar", "pe", b_mm.get(t - 4))
                a_e = p.inc("ac")
                if diag:
                    p.on("scalar", lambda e, se_slot=se_slot, slot=slot: e.activation(
                        out=exp4[:, se_slot * 512:(se_slot + 1) * 512], in_=s16m[:],
                        func=AF.Exp, bias=bias_h[slot][:, 0:1], scale=SCALE)
                        .then_inc(p.h("ac"), 1))
                    prev_diag = a_e
                else:
                    p.on("scalar", lambda e, se_slot=se_slot, slot=slot, c=c, j=j:
                         e.activation(
                             out=exp4[:, se_slot * 512:(se_slot + 1) * 512],
                             in_=s16[c % 2][:, j * 512:(j + 1) * 512],
                             func=AF.Exp, bias=bias_h[slot][:, 0:1], scale=SCALE)
                         .then_inc(p.h("ac"), 1))
                b_exp[t] = a_e
                s16_rd[(c, j)] = ("dv", v_m) if diag else ("ac", a_e)
                p.wait("tensor", "ac", a_e)
                first = (j == 0)
                lastj = (j == n_all - 1)
                p.on("tensor", lambda e, j=j, se_slot=se_slot, first=first, lastj=lastj,
                     ms=ms: e.matmul(
                         ms[0][:], selb[:, j * 128:(j + 1) * 128],
                         exp4[:, se_slot * 512:(se_slot + 1) * 512],
                         start=first, stop=lastj))
                v_mm = p.inc("pe")
                p.on("tensor", lambda e, j=j, g=g, se_slot=se_slot, first=first,
                     lastj=lastj, ms=ms: e.matmul(
                         ms[1][:],
                         vsb[:, (g * 16 + j) * 128:(g * 16 + j + 1) * 128],
                         exp4[:, se_slot * 512:(se_slot + 1) * 512],
                         start=first, stop=lastj)
                     .then_inc(p.h("pe"), 1))
                b_mm[t] = v_mm
                runB[0] = t + 1
            h = g * 4 + h4
            p.wait("vector", "pe", b_mm[runB[0] - 1])
            v_rc = p.inc("dv")
            p.on("vector", lambda e, ms=ms: e.reciprocal(out=recip[:], in_=ms[0][:])
                 .then_inc(p.h("dv"), 1))
            p.wait("vector", "dv", v_rc)
            v_fin = p.inc("dv")
            p.on("vector", lambda e, h=h, c=c, ms=ms: e.tensor_mul(
                out=an_sb[:, h * TQ + c * 512:h * TQ + (c + 1) * 512],
                in0=ms[1][:], in1=recip[:]).then_inc(p.h("dv"), 1))
            return v_fin

        prefetch_qh(0)
        prefetch_qh(1)
        ms_free = {0: 0, 1: 0}
        mi = 0
        for g in range(KVH):
            if g == 0:
                emit_A(0, 0, 0)
                emit_A(0, 0, 1)
                emit_bias(0, 0)
            for h4 in range(4):
                nh_g, nh_h = (g, h4 + 1) if h4 < 3 else (g + 1, 0)
                for c in range(2):
                    mb = mi % 2
                    p.wait("tensor", "dv", ms_free[mb])
                    ms = (ps_sum[mb], ps_pv[mb])
                    v_fin = emit_B(g, h4, c, ms)
                    ms_free[mb] = v_fin
                    mi += 1
                    if nh_g < KVH:
                        emit_A(nh_g, nh_h, c)
                if nh_g < KVH:
                    emit_bias(nh_g, nh_h)
            prefetch_qh(g + 2)
    p.barrier()
    kv_stack.close()
    dbg_dump_sbuf("an_sb", an_sb, (128, H * TQ), BF16)
    if max_phase <= 5:
        return _finish(nc, p, ctx)

    branch_tok = ctx.enter_context(nc.sbuf_tensor("br_tok", [128, 8 * D], BF16,
                                                  side="left"))
    branch_v3 = branch_tok.ap().rearrange("p (a r) -> p a r", a=8)

    with contextlib.ExitStack() as octx:
        co = [octx.enter_context(nc.sbuf_tensor(f"co{i}", [128, D], BF16, side="right"))
              for i in range(2)]
        bo = [octx.enter_context(nc.sbuf_tensor(f"bo{i}", [128, D], BF16, side="right"))
              for i in range(2)]
        wo = [octx.enter_context(nc.sbuf_tensor(f"wo{i}", [128, D], BF16, side="right"))
              for i in range(2)]
        yt_sb = [octx.enter_context(nc.sbuf_tensor(f"yt{i}", [128, TQ], F32, side="right"))
                 for i in range(2)]
        sfm = octx.enter_context(nc.sbuf_tensor("sfm", [128, TQ], BF16, side="right"))
        ps_o = [octx.enter_context(nc.psum_tensor(f"pso{i}", [128, 512], F32))
                for i in range(4)]
        ps_bt = [octx.enter_context(nc.psum_tensor(f"psbt{i}", [128, 1024], BF16))
                 for i in range(2)]

        o_dq = {}
        o_mm = {}
        o_bt = {}
        o_sfm = {}
        for dm in range(KT):
            bi = dm % 2
            p.wait("gpsimd", "dv", o_dq.get(dm - 2))
            sc_, sb_ = p.dsem(f"co{bi}"), p.dsem(f"bo{bi}")
            gC = p.inc(sc_, 16)
            p.on("gpsimd", lambda e, dm=dm, bi=bi, sc_=sc_: e.dma_start(
                out=co[bi][:], in_=o_codes[dm * 128:(dm + 1) * 128, :])
                .then_inc(p.h(sc_), 16))
            gS = p.inc(sb_, 16)
            p.on("gpsimd", lambda e, dm=dm, bi=bi, sb_=sb_: e.dma_start(
                out=bo[bi][:], in_=o_scales[dm:dm + 1, :].broadcast_to([128, D]))
                .then_inc(p.h(sb_), 16))
            p.wait("vector", sc_, gC)
            p.wait("vector", sb_, gS)
            p.wait("vector", "pe", o_mm.get(dm - 2))
            v_dq = p.inc("dv")
            p.on("vector", lambda e, bi=bi: e.scalar_tensor_tensor(
                out=wo[bi][:], in0=co[bi][:], scalar=-8.0, in1=bo[bi][:],
                op0=ALU.add, op1=ALU.mult).then_inc(p.h("dv"), 1))
            o_dq[dm] = v_dq
            p.wait("sync", "dv", o_sfm.get(dm - 2))
            syt = p.dsem(f"yt{bi}")
            sxp = p.inc(syt, 16)
            p.on("sync", lambda e, dm=dm, bi=bi, syt=syt: e.dma_start(
                out=yt_sb[bi][:], in_=yT_dram[dm * 128:(dm + 1) * 128, :])
                .then_inc(p.h(syt), 16))
            p.wait("tensor", "dv", v_dq)
            p.wait("tensor", "dv", o_sfm.get(dm - 2))
            v_mm = None
            for a in range(KT):
                for tc in range(2):
                    bank = ps_o[(bi % 2) * 2 + tc]
                    def mm_fn(e, bi=bi, a=a, tc=tc, bank=bank):
                        return e.matmul(
                            bank[:], wo[bi][:, a * 128:(a + 1) * 128],
                            an_sb[:, a * TQ + tc * 512:a * TQ + (tc + 1) * 512],
                            start=(a == 0), stop=(a == KT - 1))
                    if a == KT - 1 and tc == 1:
                        v_mm = p.inc("pe")
                        p.on("tensor", lambda e, f=mm_fn: f(e).then_inc(p.h("pe"), 1))
                    else:
                        p.on("tensor", mm_fn)
            o_mm[dm] = v_mm
            p.wait("vector", "pe", v_mm)
            p.wait("vector", syt, sxp)
            p.wait("vector", "pe", o_bt.get(dm - 1, None) and None or None)
            v_s = None
            for tc in range(2):
                v_s = p.inc("dv")
                p.on("vector", lambda e, bi=bi, tc=tc: e.tensor_add(
                    out=sfm[:, tc * 512:(tc + 1) * 512], in0=ps_o[(bi % 2) * 2 + tc][:],
                    in1=yt_sb[bi][:, tc * 512:(tc + 1) * 512])
                    .then_inc(p.h("dv"), 1))
            o_sfm[dm] = v_s
            p.wait("tensor", "dv", v_s)
            p.wait("tensor", "dv", o_bt.get(dm - 2))
            v_pe = None
            for i in range(8):
                def t_fn(e, i=i, bi=bi):
                    return e.transpose(ps_bt[bi][:, i * 128:(i + 1) * 128],
                                       sfm[:, i * 128:(i + 1) * 128], ident[:])
                if i < 7:
                    p.on("tensor", t_fn)
                else:
                    v_pe = p.inc("pe")
                    p.on("tensor", lambda e, f=t_fn: f(e).then_inc(p.h("pe"), 1))
            p.wait("vector", "pe", v_pe)
            v_bt = p.inc("dv")
            def bt_fn(e, dm=dm, bi=bi):
                src = ps_bt[bi][:, 0:1024].rearrange("p (a n) -> p a n", a=8)
                d = branch_v3[:, :, dm * 128:(dm + 1) * 128]
                return e.tensor_copy(out=d, in_=src).then_inc(p.h("dv"), 1)
            p.on("vector", bt_fn)
            o_bt[dm] = v_bt
    p.barrier()
    if max_phase <= 6:
        return _finish(nc, p, ctx)

    with contextlib.ExitStack() as zctx:
        xq = [zctx.enter_context(nc.sbuf_tensor(f"xq{i}", [128, D], F32, side="right"))
              for i in range(2)]
        ob = [zctx.enter_context(nc.sbuf_tensor(f"ob{i}", [128, D], F32, side="right"))
              for i in range(2)]
        prev_add = {}
        prev_st = {}
        for tt in range(8):
            bi = tt % 2
            sxq, sos = p.dsem(f"xq{bi}"), p.dsem(f"os{bi}")
            p.wait("sync", "dv", prev_add.get(tt - 2))
            sxx = p.inc(sxq, 16)
            p.on("sync", lambda e, tt=tt, bi=bi, sxq=sxq: e.dma_start(
                out=xq[bi][:], in_=x_kv[1024 + tt * 128:1024 + (tt + 1) * 128, :])
                .then_inc(p.h(sxq), 16))
            p.wait("vector", sxq, sxx)
            p.wait("vector", sos, prev_st.get(tt - 2))
            v_o = p.inc("dv")
            p.on("vector", lambda e, tt=tt, bi=bi: e.tensor_add(
                out=ob[bi][:], in0=xq[bi][:], in1=branch_tok[:, tt * D:(tt + 1) * D])
                .then_inc(p.h("dv"), 1))
            prev_add[tt] = v_o
            p.wait("sync", "dv", v_o)
            v_st = p.inc(sos, 16)
            p.on("sync", lambda e, tt=tt, bi=bi, sos=sos: e.dma_start(
                out=out[tt * 128:(tt + 1) * 128, :], in_=ob[bi][:])
                .then_inc(p.h(sos), 16))
            prev_st[tt] = v_st
        p.on("sync", lambda e: e.wait_ge(p.h("os0"), p.cnt["os0"]))
        p.on("sync", lambda e: e.wait_ge(p.h("os1"), p.cnt["os1"]))

    return _finish(nc, p, ctx)


def _host_inputs(core, x):
    b, half = core // 2, core % 2
    if half == 1:
        xkv = np.ascontiguousarray(x[b])
    else:
        xkv = np.concatenate([np.zeros((1024, D), np.float32),
                              np.ascontiguousarray(x[b, 0:1024])], axis=0)
    if half == 1:
        pos = np.arange(TKV, dtype=np.float64)
    else:
        pos = np.maximum(np.arange(TKV, dtype=np.float64) - 1024.0, 0.0)
    inv = 1.0 / (10000.0 ** (np.arange(64, dtype=np.float64) / 64.0))
    ang = pos[None, :] * inv[:, None]
    cos = np.cos(ang)
    sin = np.sin(ang)
    ropev = np.zeros((128, 2 * TKV), dtype=BF)
    ropev[0:64, 0:TKV] = cos.astype(BF)
    ropev[64:128, 0:TKV] = cos.astype(BF)
    ropev[0:64, TKV:] = sin.astype(BF)
    ropev[64:128, TKV:] = sin.astype(BF)
    selv = np.zeros((128, TKV), dtype=BF)
    for j in range(16):
        real = (half == 1) or (j >= 8)
        selv[:, j * 128:(j + 1) * 128] = np.float32(4.0 if real else 0.0)
    return {"x_kv": xkv, "rope": ropev, "sel": selv}


def _attn_layouts(o_codes, o_scales):
    oc = np.ascontiguousarray(
        o_codes.reshape(32, 128, 32, 128).transpose(2, 1, 0, 3).reshape(4096, 4096))
    os2 = np.ascontiguousarray(
        o_scales.reshape(32, 32, 128).transpose(1, 0, 2).reshape(32, 4096))
    return oc, os2


def _ffn_layouts(Wg, Wu, Wd):
    Wu2 = np.ascontiguousarray(
        Wu.reshape(86, 128, 32, 128).transpose(0, 3, 2, 1).reshape(11008, 4096))
    Wg2 = np.ascontiguousarray(
        Wg.reshape(86, 128, 32, 128).transpose(0, 3, 2, 1).reshape(11008, 4096))
    Wd2 = np.ascontiguousarray(
        Wd.reshape(32, 128, 86, 128).transpose(0, 3, 2, 1).reshape(4096, 11008))
    return Wg2, Wu2, Wd2


def kernel(L0_attn_x, L0_attn_q_codes, L0_attn_k_codes, L0_attn_v_codes,
           L0_attn_o_codes, L0_attn_q_scales, L0_attn_k_scales, L0_attn_v_scales,
           L0_attn_o_scales, L0_ffn_W_gate, L0_ffn_W_up, L0_ffn_W_down):
    x = np.asarray(L0_attn_x, dtype=np.float32)
    shared = {
        "q_codes": np.ascontiguousarray(L0_attn_q_codes, dtype=np.int32),
        "k_codes": np.ascontiguousarray(L0_attn_k_codes, dtype=np.int32),
        "v_codes": np.ascontiguousarray(L0_attn_v_codes, dtype=np.int32),
        "o_codes": np.ascontiguousarray(L0_attn_o_codes, dtype=np.int32),
        "q_scales": np.ascontiguousarray(L0_attn_q_scales, dtype=np.float32),
        "k_scales": np.ascontiguousarray(L0_attn_k_scales, dtype=np.float32),
        "v_scales": np.ascontiguousarray(L0_attn_v_scales, dtype=np.float32),
        "o_scales": np.ascontiguousarray(L0_attn_o_scales, dtype=np.float32),
    }
    wg2, wu2, wd2 = _ffn_layouts(np.asarray(L0_ffn_W_gate, dtype=np.float32),
                                 np.asarray(L0_ffn_W_up, dtype=np.float32),
                                 np.asarray(L0_ffn_W_down, dtype=np.float32))
    oc2, os2 = _attn_layouts(shared["o_codes"], shared["o_scales"])
    shared["o_codes"] = oc2
    shared["o_scales"] = os2
    shared["w_gate"] = wg2
    shared["w_up"] = wu2
    shared["w_down"] = wd2
    nc = build_program()
    in_maps = []
    for core in range(8):
        m = dict(shared)
        m.update(_host_inputs(core, x))
        in_maps.append(m)
    res = run_bass_kernel_spmd(nc, in_maps, core_ids=list(range(8)))
    full = np.empty((4, 2048, D), dtype=np.float32)
    for b in range(4):
        full[b, 0:1024] = res.results[2 * b]["out"]
        full[b, 1024:2048] = res.results[2 * b + 1]["out"]
    return full
